# version 13
# baseline (speedup 1.0000x reference)
import numpy as np
from contextlib import ExitStack
import concourse.bass as bass
import concourse.mybir as mybir
from concourse.bass_utils import run_bass_kernel_spmd

F32 = mybir.dt.float32
BF16 = mybir.dt.bfloat16
U8 = mybir.dt.uint8
AF = mybir.ActivationFunctionType
ALU = mybir.AluOpType

D = 2048
II = 8192
T = 2048
NT = 16
NE = 16
ND = 16
NIC = 64
SBT = 1024
TB = 512
DENOM = float(np.sqrt(np.float32(2048.0)) / np.float32(8192.0))
EPS = 1e-5

DEBUG = False
_NO_CC = False
_PHASE = 5


class Sem:
    __slots__ = ("h", "n")

    def __init__(self, h):
        self.h = h
        self.n = 0


class Build:
    ENGS = ("sync", "gpsimd", "scalar", "vector", "tensor")

    def __init__(self):
        self.ops = {e: [] for e in self.ENGS}
        self._waited = {e: {} for e in self.ENGS}

    def wait(self, eng, sem, val):
        if val is None or val <= 0:
            return
        w = self._waited[eng]
        if w.get(id(sem), -1) >= val:
            return
        w[id(sem)] = val
        self.ops[eng].append(lambda e, h=sem.h, v=val: e.wait_ge(h, v))

    def waitp(self, eng, pair):
        if pair is not None:
            self.wait(eng, pair[0], pair[1])

    def do(self, eng, fn, sem=None, k=1):
        if sem is None:
            self.ops[eng].append(fn)
            return None
        sem.n += k
        tgt = sem.n
        self.ops[eng].append(lambda e, f=fn, h=sem.h, kk=k: f(e).then_inc(h, kk))
        return tgt

    def inc_last(self, eng, sem, k=1):
        sem.n += k
        tgt = sem.n
        prev = self.ops[eng].pop()
        self.ops[eng].append(lambda e, p=prev, h=sem.h, kk=k: p(e).then_inc(h, kk))
        return tgt

    def barrier(self, sems):
        snap = [(s, s.n) for s in sems]
        for eng in self.ENGS:
            for s, v in snap:
                self.wait(eng, s, v)

    def install(self, block):
        for name in self.ENGS:
            ops = self.ops[name]

            def body(e, ops=ops):
                for fn in ops:
                    fn(e)

            getattr(block, name)(body)


class DmaPool:

    def __init__(self, B, sems):
        self.B = B
        self.sems = sems
        self.i = 0

    def issue(self, eng, fn):
        s = self.sems[self.i % len(self.sems)]
        self.i += 1
        self.B.wait(eng, s, s.n)
        tgt = self.B.do(eng, fn, s, 16)
        return (s, tgt)


def _build_nc():
    nc = bass.Bass("TRN2", target_bir_lowering=False, detect_race_conditions=False)

    d_x = nc.dram_tensor("x", [T, D], F32, kind="ExternalInput").ap()
    d_wqT = nc.dram_tensor("wqT", [16, 128, 2048], F32, kind="ExternalInput").ap()
    d_wkT = nc.dram_tensor("wkT", [16, 128, 2048], F32, kind="ExternalInput").ap()
    d_wvT = nc.dram_tensor("wvT", [16, 128, 2048], F32, kind="ExternalInput").ap()
    d_woT = nc.dram_tensor("woT", [8, 128, 4096], F32, kind="ExternalInput").ap()
    d_wgT = nc.dram_tensor("wgT", [32, 128, 4096], U8, kind="ExternalInput").ap()
    d_wuT = nc.dram_tensor("wuT", [32, 128, 4096], U8, kind="ExternalInput").ap()
    d_wdT = nc.dram_tensor("wdT", [2, 16, 128, 4096], U8, kind="ExternalInput").ap()
    d_sg = nc.dram_tensor("sg", [128, NIC], F32, kind="ExternalInput").ap()
    d_su = nc.dram_tensor("su", [128, NIC], F32, kind="ExternalInput").ap()
    d_sd = nc.dram_tensor("sd", [128, D], F32, kind="ExternalInput").ap()
    d_mask = nc.dram_tensor("mask", [128, 8], F32, kind="ExternalInput").ap()
    d_out = nc.dram_tensor("out", [T, D], F32, kind="ExternalOutput").ap()

    s_kind = "ExternalOutput" if DEBUG else "Internal"
    d_sscr = nc.dram_tensor("s_scr", [T, D], BF16, kind=s_kind).ap()
    t_ccin = nc.dram_tensor("cc_in", [128, NE], F32)
    t_ccout = nc.dram_tensor("cc_out", [128 * 8, NE], F32)
    if DEBUG:
        d_dbg_xnT = nc.dram_tensor("dbg_xnT", [128, NE * T], BF16, kind="ExternalOutput").ap()
        d_dbg_kv = nc.dram_tensor("dbg_kv", [128, NE * T], BF16, kind="ExternalOutput").ap()
        d_dbg_carry = nc.dram_tensor("dbg_carry", [128, NE], F32, kind="ExternalOutput").ap()
        d_dbg_tot = nc.dram_tensor("dbg_tot", [128, NE], F32, kind="ExternalOutput").ap()
        d_dbg_q0 = nc.dram_tensor("dbg_q0", [128, NE * SBT], BF16, kind="ExternalOutput").ap()

    B = Build()
    es = ExitStack()
    with es:
        E = es.enter_context
        pe = Sem(E(nc.semaphore("pe")))
        sc = Sem(E(nc.semaphore("sc")))
        ve = Sem(E(nc.semaphore("ve")))
        gp = Sem(E(nc.semaphore("gp")))
        cc = Sem(E(nc.semaphore("cc")))
        SP = DmaPool(B, [Sem(E(nc.semaphore(f"dp{i}"))) for i in range(12)])
        GP = DmaPool(B, [Sem(E(nc.semaphore(f"dq{i}"))) for i in range(12)])
        allsems = [pe, sc, ve, gp] + SP.sems + GP.sems

        ident = E(nc.sbuf_tensor("ident", [128, 128], BF16))
        zeros = E(nc.sbuf_tensor("zeros", [128, SBT], BF16))
        sgb = E(nc.sbuf_tensor("sgb", [128, NIC], F32))
        sub = E(nc.sbuf_tensor("sub", [128, NIC], F32))
        maskb = E(nc.sbuf_tensor("maskb", [128, 8], F32))
        tot0 = E(nc.sbuf_tensor("tot0", [128, NE], F32))
        tot1 = E(nc.sbuf_tensor("tot1", [128, NE], F32))
        totA = E(nc.sbuf_tensor("totA", [128, NE], F32))
        gath = E(nc.sbuf_tensor("gath", [128, 8 * NE], F32))
        carry = E(nc.sbuf_tensor("carry", [128, NE], F32))
        carry2 = E(nc.sbuf_tensor("carry2", [128, NE], F32))
        stats = E(nc.sbuf_tensor("stats", [128, 16], F32))
        xnT = E(nc.sbuf_tensor("xnT", [128, NE * T], BF16))

        AB = E(nc.sbuf_tensor("AB", [128, 59392], BF16))
        AFa = E(nc.sbuf_tensor("AF", [128, 5120], F32))
        P = [E(nc.psum_tensor(f"P{i}", [128, 1024], F32)) for i in range(4)]


        zeros_rdy = B.do("gpsimd", lambda e: e.memset(zeros[:], 0.0), gp, 1)
        ms2 = B.do("gpsimd", lambda e: e.memset(ident[:], 0.0), gp, 1)
        B.wait("gpsimd", gp, ms2)
        ident_rdy = B.do("gpsimd", lambda e: e.affine_select(
            out=ident[:], in_=ident[:], compare_op=ALU.not_equal,
            fill=1.0, base=0, pattern=[[-1, 128]], channel_multiplier=1), gp, 1)

        xln = AFa[:, 0:4096]
        sqscr = AB[:, 0:2048]

        x_ld = [None] * NT
        ln_vA = [None] * NT
        ln_vC = [None] * NT
        ln_scD = [None] * NT
        tr_pe = [[None, None] for _ in range(NT)]
        tr_ve = [[None, None] for _ in range(NT)]

        xnT_r = xnT[:, :].rearrange("p (c n) -> p c n", n=T)

        kv = AB[:, 0:32768]
        qv = AB[:, 32768:49152]
        wsl = AB[:, 49152:57344]
        scol = AB[:, 57344:59392]
        ksb = AB[:, 32768:34816]
        kvt = AFa[:, 2048:3072]
        pf8 = [None] * 8
        wfree = [None, None]
        wqfree = [None, None, None, None]
        ring = [0]
        ring8 = [0]

        def pwait_full(eng, r):
            B.waitp(eng, pf8[2 * r])
            B.waitp(eng, pf8[2 * r + 1])

        def pset_full(r, pair):
            pf8[2 * r] = pair
            pf8[2 * r + 1] = pair
        stt_tgt = {}
        kvseq = [0]
        kvA = [0]
        xnT_sb0_ve = [None]

        def load_wslice(b, dram, sl, pe_free):
            B.waitp("gpsimd", pe_free)
            src = dram[sl, :, :]
            dst = wsl[:, b * 4096:(b + 1) * 4096]
            return GP.issue("gpsimd", lambda e, s=src, d=dst: e.dma_start(out=d, in_=s))

        def load_wslot(slot, dram, sl, pe_free):
            B.waitp("gpsimd", pe_free)
            src = dram[sl, :, :]
            dst = wsl[:, slot * 2048:(slot + 1) * 2048]
            return GP.issue("gpsimd", lambda e, s=src, d=dst: e.dma_start(out=d, in_=s))

        def emit_kv_chunk(ec, sb, r0, r1):
            seq = kvseq[0]; kvseq[0] += 1
            kslot = seq % 2
            vslot = 2 + seq % 2
            wk_rdy = load_wslot(kslot, d_wkT, ec, wqfree[kslot])
            wv_rdy = load_wslot(vslot, d_wvT, ec, wqfree[vslot])
            koff = ec * T + sb * SBT
            B.waitp("tensor", wk_rdy)
            B.waitp("tensor", wv_rdy)
            pwait_full("tensor", r0)
            if sb == 0:
                B.waitp("tensor", xnT_sb0_ve[0])
            for dc in range(ND):
                lhs = wsl[:, kslot * 2048 + dc * 128: kslot * 2048 + dc * 128 + 128]
                for h in range(2):
                    rhs = xnT[:, dc * T + sb * SBT + h * 512:
                              dc * T + sb * SBT + h * 512 + 512]
                    B.do("tensor", lambda e, o=P[r0][:, h * 512:(h + 1) * 512],
                         l=lhs, r=rhs, st=(dc == 0), sp=(dc == ND - 1):
                         e.matmul(o, l, r, start=st, stop=sp))
            k_pe = B.inc_last("tensor", pe, 1)
            pwait_full("tensor", r1)
            for dc in range(ND):
                lhs = wsl[:, vslot * 2048 + dc * 128: vslot * 2048 + dc * 128 + 128]
                for h in range(2):
                    rhs = xnT[:, dc * T + sb * SBT + h * 512:
                              dc * T + sb * SBT + h * 512 + 512]
                    B.do("tensor", lambda e, o=P[r1][:, h * 512:(h + 1) * 512],
                         l=lhs, r=rhs, st=(dc == 0), sp=(dc == ND - 1):
                         e.matmul(o, l, r, start=st, stop=sp))
            v_pe = B.inc_last("tensor", pe, 1)
            wqfree[kslot] = (pe, v_pe)
            wqfree[vslot] = (pe, v_pe)

            kview = ksb[:, (seq % 2) * 1024:(seq % 2) * 1024 + 1024]
            B.wait("scalar", pe, k_pe)
            B.wait("scalar", sc, sc.n)
            if seq >= 2:
                B.wait("scalar", ve, stt_tgt[seq - 2])
            kev = B.do("scalar", lambda e, kv_=kview, r0=r0:
                       e.copy(kv_, P[r0][:, :]), sc, 1)
            pset_full(r0, (sc, kev))

            B.wait("vector", pe, v_pe)
            B.wait("vector", sc, kev)
            B.wait("vector", ve, ve.n)
            tt = tot0 if sb == 0 else tot1
            stt = B.do("vector", lambda e, kv_=kview, r1=r1, koff=koff,
                       tt=tt, ec=ec: e.scalar_tensor_tensor(
                           kv[:, koff:koff + SBT], kv_, 1.0, P[r1][:, :],
                           ALU.mult, ALU.mult,
                           accum_out=tt[:, ec:ec + 1]), ve, 1)
            pset_full(r1, (ve, stt))
            stt_tgt[seq] = stt

        def vstep(fn):
            B.wait("vector", ve, ve.n)
            return B.do("vector", fn, ve, 1)

        def ln_views(t):
            xb = xln[:, (t % 2) * D:(t % 2) * D + D]
            xnbf = AB[:, 2048 + (t % 2) * 2048: 2048 + (t % 2) * 2048 + 2048]
            sb8 = (t % 2) * 8
            st = [stats[:, sb8 + i:sb8 + i + 1] for i in range(8)]
            return xb, xnbf, st

        def ln_stage1(t):
            xb, xnbf, st = ln_views(t)
            s2 = st[1]
            if t >= 2:
                B.wait("sync", sc, ln_scD[t - 2])
            x_ld[t] = SP.issue("sync", lambda e, t=t, xb=xb: e.dma_start(
                out=xb, in_=d_x[t * 128:(t + 1) * 128, :]))
            B.waitp("scalar", x_ld[t])
            B.wait("scalar", sc, sc.n)
            if t >= 2:
                B.wait("scalar", ve, ln_vA[t - 2])
            ln_sq2[t] = B.do("scalar", lambda e, xb=xb, s2=s2: e.activation(
                sqscr, xb, AF.Square, accum_out=s2), sc, 1)

        def ln_stageA(t):
            xb, xnbf, st = ln_views(t)
            s1, s2, mcol, t2c, varc = st[0], st[1], st[2], st[3], st[4]
            B.waitp("vector", x_ld[t])
            if t >= 2:
                B.wait("vector", sc, ln_scD[t - 2])
            vstep(lambda e, xb=xb, s1=s1: e.reduce_sum(s1, xb, axis=mybir.AxisListType.X))
            vstep(lambda e, mcol=mcol, s1=s1: e.tensor_scalar_mul(mcol, s1, 1.0 / D))
            B.wait("vector", sc, ln_sq2[t])
            vstep(lambda e, t2c=t2c, s2=s2: e.tensor_scalar_mul(t2c, s2, 1.0 / D))
            vstep(lambda e, varc=varc, mcol=mcol: e.tensor_mul(varc, mcol, mcol))
            vstep(lambda e, varc=varc, t2c=t2c: e.tensor_sub(varc, t2c, varc))
            ln_vA[t] = vstep(lambda e, varc=varc: e.tensor_scalar_add(varc, varc, EPS))
            B.wait("scalar", ve, ln_vA[t])
            B.wait("scalar", sc, sc.n)
            ln_sB[t] = B.do("scalar", lambda e, sdc=st[5], varc=varc:
                            e.sqrt(sdc, varc), sc, 1)

        def ln_stageB(t):
            xb, xnbf, st = ln_views(t)
            mcol, sdc, rstd, nbc = st[2], st[5], st[6], st[7]
            B.wait("vector", sc, ln_sB[t])
            vstep(lambda e, rstd=rstd, sdc=sdc: e.reciprocal(rstd, sdc))
            ln_vC[t] = vstep(lambda e, nbc=nbc, mcol=mcol, rstd=rstd: e.scalar_tensor_tensor(
                nbc, mcol, -1.0, rstd, ALU.mult, ALU.mult))

            B.wait("scalar", ve, ln_vC[t])
            B.wait("scalar", sc, sc.n)
            if t >= 2:
                B.wait("scalar", pe, tr_pe[t - 2][1])
            ln_scD[t] = B.do("scalar", lambda e, xb=xb, xnbf=xnbf, nbc=nbc,
                             rstd=rstd: e.activation(
                xnbf, xb, AF.Identity, bias=nbc, scale=rstd), sc, 1)

            B.wait("tensor", sc, ln_scD[t])
            B.wait("tensor", gp, ident_rdy)
            for g in range(2):
                if t >= 1:
                    B.waitp("tensor", tr_ve[t - 1][g])
                for j in range(8):
                    c = g * 8 + j
                    B.do("tensor", lambda e, g=g, j=j, c=c, xnbf=xnbf: e.matmul(
                        P[g][:, j * 128:(j + 1) * 128],
                        xnbf[:, c * 128:(c + 1) * 128], ident[:],
                        start=True, stop=True))
                tr_pe[t][g] = B.inc_last("tensor", pe, 1)

                out_ap = xnT_r[:, g * 8:(g + 1) * 8, t * 128:(t + 1) * 128]
                in_ap = P[g][:, 0:1024].rearrange("p (j n) -> p j n", n=128)
                if g == 0:
                    B.wait("vector", pe, tr_pe[t][g])
                    B.wait("vector", ve, ve.n)
                    tr_ve[t][g] = (ve, B.do("vector", lambda e, o=out_ap, i=in_ap:
                                            e.tensor_copy(o, i), ve, 1))
                else:
                    B.wait("scalar", pe, tr_pe[t][g])
                    B.wait("scalar", sc, sc.n)
                    tr_ve[t][g] = (sc, B.do("scalar", lambda e, o=out_ap, i=in_ap:
                                            e.copy(o, i), sc, 1))

            if t == 7:
                xnT_sb0_ve[0] = tr_ve[7][1]
            if t >= 8 and _PHASE >= 2:
                emit_kv_chunk(8 + kvA[0], 0, 2, 3)
                kvA[0] += 1

        ln_sq2 = [None] * NT
        ln_sB = [None] * NT
        ln_stage1(0)
        ln_stage1(1)
        ln_stageA(0)
        for t in range(NT):
            ln_stageB(t)
            if t + 2 < NT:
                ln_stage1(t + 2)
            if t + 1 < NT:
                ln_stageA(t + 1)
        sg_ld = SP.issue("sync", lambda e: e.dma_start(out=sgb[:], in_=d_sg))
        su_ld = SP.issue("sync", lambda e: e.dma_start(out=sub[:], in_=d_su))
        mask_ld = SP.issue("sync", lambda e: e.dma_start(out=maskb[:], in_=d_mask))

        if _PHASE < 2:
            for s_ in SP.sems + GP.sems:
                B.wait("sync", s_, s_.n)
            with nc.Block() as block:
                B.install(block)
            return nc

        for ec in range(8):
            emit_kv_chunk(ec, 0, 2, 3)

        pset_full(0, tr_ve[NT - 1][0])
        pset_full(1, tr_ve[NT - 1][1])
        for ec in range(NE):
            r0 = ring[0] % 4; ring[0] += 1
            r1 = ring[0] % 4; ring[0] += 1
            emit_kv_chunk(ec, 1, r0, r1)
        wfree[0] = (pe, pe.n)
        wfree[1] = (pe, pe.n)

        if _PHASE < 3:
            for s_ in SP.sems + GP.sems:
                B.wait("sync", s_, s_.n)
            B.barrier(allsems)
            with nc.Block() as block:
                B.install(block)
            return nc

        B.wait("vector", ve, ve.n)
        totA_ve = B.do("vector", lambda e: e.tensor_add(totA[:], tot0[:], tot1[:]), ve, 1)
        B.wait("sync", ve, totA_ve)
        ccw = SP.issue("sync", lambda e: e.dma_start(out=t_ccin.ap(), in_=totA[:]))
        B.waitp("gpsimd", ccw)
        if not _NO_CC:
            B.do("gpsimd", lambda e: e.collective_compute(
                "AllGather", ALU.bypass, replica_groups=[list(range(8))],
                ins=[t_ccin.ap().opt()], outs=[t_ccout.ap().opt()]), cc, 1)
        carry_ve = None

        if _PHASE < 4:
            for s_ in SP.sems + GP.sems:
                B.wait("sync", s_, s_.n)
            B.barrier(allsems)
            with nc.Block() as block:
                B.install(block)
            return nc

        s_wr = [None] * 16
        wo_done_pe = None
        for sb in range(2):
            wqfree[0] = wqfree[1] = wfree[0]
            wqfree[2] = wqfree[3] = wfree[1]
            qv_sc = [None] * NE
            for sl in range(NE):
                slot = sl % 4
                w_rdy = load_wslot(slot, d_wqT, sl, wqfree[slot])
                ec = sl
                r0 = ring[0] % 4; ring[0] += 1
                B.waitp("tensor", w_rdy)
                pwait_full("tensor", r0)
                for dc in range(ND):
                    lhs = wsl[:, slot * 2048 + dc * 128: slot * 2048 + dc * 128 + 128]
                    for h in range(2):
                        rhs = xnT[:, dc * T + sb * SBT + h * 512:
                                  dc * T + sb * SBT + h * 512 + 512]
                        B.do("tensor", lambda e, o=P[r0][:, h * 512:(h + 1) * 512],
                             l=lhs, r=rhs, st=(dc == 0), sp=(dc == ND - 1):
                             e.matmul(o, l, r, start=st, stop=sp))
                q_pe = B.inc_last("tensor", pe, 1)
                wqfree[slot] = (pe, q_pe)
                B.wait("scalar", pe, q_pe)
                B.wait("scalar", sc, sc.n)
                if ec < 2:
                    B.wait("scalar", ve, max(stt_tgt.values()))
                if sb == 1:
                    B.wait("scalar", pe, wo_done_pe)
                qv_sc[ec] = B.do("scalar", lambda e, ec=ec, r0=r0: e.activation(
                    qv[:, ec * SBT:(ec + 1) * SBT], P[r0][:, :],
                    AF.Identity, scale=DENOM), sc, 1)
                pset_full(r0, (sc, qv_sc[ec]))
            wfree[0] = (pe, pe.n)
            wfree[1] = (pe, pe.n)

            if DEBUG and sb == 0:
                B.wait("sync", sc, qv_sc[NE - 1])
                SP.issue("sync", lambda e: e.dma_start(out=d_dbg_q0, in_=qv))

            if sb == 0:
                B.wait("sync", cc, 1)
                g_lds = []
                for r in range(8):
                    g_lds.append(SP.issue("sync", lambda e, r=r: e.dma_start(
                        out=gath[:, r * NE:(r + 1) * NE],
                        in_=t_ccout.ap()[r * 128:(r + 1) * 128, :])))
                for t_ in g_lds:
                    B.waitp("vector", t_)
                B.waitp("vector", mask_ld)
                B.wait("vector", ve, ve.n)
                B.do("vector", lambda e: e.tensor_scalar(
                    carry[:], gath[:, 0:NE], maskb[:, 0:1], None, ALU.mult), ve, 1)
                for r in range(1, 8):
                    B.wait("vector", ve, ve.n)
                    carry_ve = B.do("vector", lambda e, r=r: e.scalar_tensor_tensor(
                        carry[:], gath[:, r * NE:(r + 1) * NE], maskb[:, r:r + 1],
                        carry[:], ALU.mult, ALU.add), ve, 1)

            vhat_ve = None
            B.wait("vector", gp, zeros_rdy)
            B.wait("vector", ve, max(stt_tgt.values()))
            if carry_ve:
                B.wait("vector", ve, carry_ve)
            for ec in range(NE):
                koff = ec * T + sb * SBT
                init = carry[:, ec:ec + 1] if sb == 0 else carry2[:, ec:ec + 1]
                B.wait("vector", sc, qv_sc[ec])
                B.wait("vector", ve, ve.n)
                B.do("vector", lambda e, koff=koff, init=init: e.tensor_tensor_scan(
                    kvt, kv[:, koff:koff + SBT], zeros[:, :], init,
                    ALU.add, ALU.add), ve, 1)
                if sb == 0:
                    B.wait("vector", ve, ve.n)
                    B.do("vector", lambda e, ec=ec: e.tensor_copy(
                        carry2[:, ec:ec + 1], kvt[:, SBT - 1:SBT]), ve, 1)
                B.wait("vector", ve, ve.n)
                vhat_ve = B.do("vector", lambda e, ec=ec: e.tensor_mul(
                    qv[:, ec * SBT:(ec + 1) * SBT],
                    qv[:, ec * SBT:(ec + 1) * SBT], kvt), ve, 1)

            B.wait("tensor", ve, vhat_ve)
            for sl in range(8):
                w_rdy = load_wslice(sl % 2, d_woT, sl, wfree[sl % 2])
                sc_evs = None
                gs = sb * 8 + sl
                for tile in range(8):
                    r8 = ring8[0] % 8; ring8[0] += 1
                    pview = P[r8 // 2][:, (r8 % 2) * 512:(r8 % 2) * 512 + 256]
                    B.waitp("tensor", w_rdy)
                    B.waitp("tensor", pf8[r8])
                    for ec in range(NE):
                        lhs = qv[:, ec * SBT + tile * 128: ec * SBT + tile * 128 + 128]
                        rhs = wsl[:, (sl % 2) * 4096 + ec * 256: (sl % 2) * 4096 + ec * 256 + 256]
                        B.do("tensor", lambda e, o=pview, l=lhs, r=rhs,
                             st=(ec == 0), sp=(ec == NE - 1):
                             e.matmul(o, l, r, start=st, stop=sp))
                    wo_pe = B.inc_last("tensor", pe, 1)
                    B.wait("scalar", pe, wo_pe)
                    B.wait("scalar", sc, sc.n)
                    if gs >= 1 and tile == 0:
                        B.waitp("scalar", s_wr[gs - 1])
                    sc_evs = B.do("scalar", lambda e, tile=tile, pview=pview:
                                  e.copy(scol[:, tile * 256: tile * 256 + 256],
                                         pview), sc, 1)
                    pf8[r8] = (sc, sc_evs)
                wfree[sl % 2] = (pe, pe.n)
                B.wait("sync", sc, sc_evs)
                dst = d_sscr[sb * SBT:(sb + 1) * SBT, sl * 256:(sl + 1) * 256] \
                    .rearrange("(tl p) f -> p tl f", p=128)
                src = scol.rearrange("p (tl f) -> p tl f", f=256)
                s_wr[gs] = SP.issue("sync", lambda e, d=dst, s=src:
                                    e.dma_start(out=d, in_=s))
            wo_done_pe = pe.n

        if DEBUG:
            B.wait("sync", ve, ve.n)
            B.wait("sync", sc, sc.n)
            SP.issue("sync", lambda e: e.dma_start(out=d_dbg_xnT, in_=xnT[:, :]))
            SP.issue("sync", lambda e: e.dma_start(out=d_dbg_kv, in_=kv))
            SP.issue("sync", lambda e: e.dma_start(out=d_dbg_carry, in_=carry[:, :]))
            SP.issue("sync", lambda e: e.dma_start(out=d_dbg_tot, in_=totA[:, :]))

        B.barrier(allsems)

        if _PHASE < 5:
            for s_ in SP.sems + GP.sems:
                B.wait("sync", s_, s_.n)
            with nc.Block() as block:
                B.install(block)
            return nc

        hbuf = AB[:, 0:32768]
        wfsl = AB[:, 32768:49152]
        shalf = AB[:, 49152:51200]
        gfin = AB[:, 51200:55296]
        mbuf = AB[:, 55296:57344]
        sdbc = AB[:, 57344:59392]
        xhalf = AFa[:, 0:2048]
        ohalf = AFa[:, 2048:4096]
        t0b = AFa[:, 4096:5120]

        ffree = [None, None, None, None]
        fslot = [0]

        sd_ld = GP.issue("gpsimd", lambda e: e.dma_start(out=sdbc, in_=d_sd))

        chain_idx = [0]
        ch_ev2 = {}; ch_ev3 = {}; ch_ev4 = {}; ch_ev5 = {}
        ch_xld = {}; ch_sld = {}
        tails = []

        def flush_tails():
            for fn in tails:
                fn()
            tails.clear()

        def emit_tail(n, gt, dh):
            if n >= 2:
                B.wait("sync", ve, ch_ev4[n - 2])
                B.wait("sync", ve, ch_ev3[n - 2])
            ch_xld[n] = SP.issue("sync", lambda e, gt=gt, dh=dh, n=n: e.dma_start(
                out=xhalf[:, (n % 2) * 1024:(n % 2) * 1024 + 1024],
                in_=d_x[gt * 128:(gt + 1) * 128, dh * 1024:(dh + 1) * 1024]))
            ch_sld[n] = SP.issue("sync", lambda e, gt=gt, dh=dh, n=n: e.dma_start(
                out=shalf[:, (n % 2) * 1024:(n % 2) * 1024 + 1024],
                in_=d_sscr[gt * 128:(gt + 1) * 128, dh * 1024:(dh + 1) * 1024]))
            B.wait("vector", sc, ch_ev2[n])
            B.waitp("vector", ch_sld[n])
            B.wait("vector", ve, ve.n)
            ch_ev3[n] = B.do("vector", lambda e, n=n: e.tensor_mul(
                mbuf[:, (n % 2) * 1024:(n % 2) * 1024 + 1024],
                gfin[:, (n % 4) * 1024:(n % 4) * 1024 + 1024],
                shalf[:, (n % 2) * 1024:(n % 2) * 1024 + 1024]), ve, 1)
            B.waitp("vector", ch_xld[n])
            if n >= 2:
                B.waitp("vector", ch_ev5[n - 2])
            B.wait("vector", ve, ve.n)
            ch_ev4[n] = B.do("vector", lambda e, n=n: e.tensor_add(
                ohalf[:, (n % 2) * 1024:(n % 2) * 1024 + 1024],
                mbuf[:, (n % 2) * 1024:(n % 2) * 1024 + 1024],
                xhalf[:, (n % 2) * 1024:(n % 2) * 1024 + 1024]), ve, 1)
            B.wait("sync", ve, ch_ev4[n])
            ch_ev5[n] = SP.issue("sync", lambda e, gt=gt, dh=dh, n=n: e.dma_start(
                out=d_out[gt * 128:(gt + 1) * 128, dh * 1024:(dh + 1) * 1024],
                in_=ohalf[:, (n % 2) * 1024:(n % 2) * 1024 + 1024]))

        for tb in range(4):
            up_last_ve = None
            hev_gate = [None] * NIC
            for mat in range(2):
                dram = d_wgT if mat == 0 else d_wuT
                scl = sgb if mat == 0 else sub

                def gate_ldwfx(sl, dram=dram):
                    b = fslot[0] % 4; fslot[0] += 1
                    B.waitp("gpsimd", ffree[b])
                    src = dram[sl, :, :]
                    dst = wfsl[:, b * 4096:(b + 1) * 4096]
                    wld = GP.issue("gpsimd", lambda e, s=src, d=dst:
                                   e.dma_start(out=d, in_=s))
                    B.waitp("vector", wld)
                    B.waitp("vector", ffree[b])
                    B.wait("vector", ve, ve.n)
                    wfx = B.do("vector", lambda e, b=b: e.tensor_scalar_add(
                        wfsl[:, b * 4096:(b + 1) * 4096],
                        wfsl[:, b * 4096:(b + 1) * 4096], -128.0), ve, 1)
                    return b, wfx

                cur = gate_ldwfx(0)
                for sl in range(32):
                    b, wfx = cur
                    cur = gate_ldwfx(sl + 1) if sl + 1 < 32 else None
                    if mat == 0 and sl == 2:
                        flush_tails()
                    last_mm = None
                    for j in range(2):
                        ic = sl * 2 + j
                        r8 = ring8[0] % 8; ring8[0] += 1
                        pview = P[r8 // 2][:, (r8 % 2) * 512:(r8 % 2) * 512 + TB]
                        B.wait("tensor", ve, wfx)
                        B.waitp("tensor", pf8[r8])
                        for dc in range(ND):
                            lhs = wfsl[:, b * 4096 + dc * 256 + j * 128:
                                       b * 4096 + dc * 256 + j * 128 + 128]
                            rhs = xnT[:, dc * T + tb * TB: dc * T + tb * TB + TB]
                            B.do("tensor", lambda e, o=pview, l=lhs, r=rhs,
                                 st=(dc == 0), sp=(dc == ND - 1):
                                 e.matmul(o, l, r, start=st, stop=sp))
                        g_pe = B.inc_last("tensor", pe, 1)
                        last_mm = g_pe
                        if mat == 0:
                            B.wait("scalar", pe, g_pe)
                            B.wait("scalar", sc, sc.n)
                            hev = B.do("scalar", lambda e, ic=ic, pview=pview, scl=scl:
                                       e.activation(hbuf[:, ic * TB:(ic + 1) * TB],
                                                    pview, AF.Silu,
                                                    scale=scl[:, ic:ic + 1]), sc, 1)
                            pf8[r8] = (sc, hev)
                            hev_gate[ic] = hev
                        else:
                            B.wait("vector", pe, g_pe)
                            B.wait("vector", sc, hev_gate[ic])
                            B.wait("vector", ve, ve.n)
                            hev = B.do("vector", lambda e, ic=ic, pview=pview, scl=scl:
                                       e.scalar_tensor_tensor(
                                           hbuf[:, ic * TB:(ic + 1) * TB],
                                           pview, scl[:, ic:ic + 1],
                                           hbuf[:, ic * TB:(ic + 1) * TB],
                                           ALU.mult, ALU.mult), ve, 1)
                            pf8[r8] = (ve, hev)
                            up_last_ve = hev
                    ffree[b] = (pe, last_mm)

            B.wait("tensor", ve, up_last_ve)
            for dh in range(2):
                for r8_ in range(8):
                    B.waitp("tensor", pf8[r8_])

                def down_ldwfx(sl, dh=dh):
                    b = fslot[0] % 4; fslot[0] += 1
                    B.waitp("gpsimd", ffree[b])
                    src = d_wdT[dh, sl, :, :]
                    dst = wfsl[:, b * 4096:(b + 1) * 4096]
                    wld = GP.issue("gpsimd", lambda e, s=src, d=dst:
                                   e.dma_start(out=d, in_=s))
                    B.waitp("vector", wld)
                    B.waitp("vector", ffree[b])
                    B.wait("vector", ve, ve.n)
                    wfx = B.do("vector", lambda e, b=b: e.tensor_scalar_add(
                        wfsl[:, b * 4096:(b + 1) * 4096],
                        wfsl[:, b * 4096:(b + 1) * 4096], -128.0), ve, 1)
                    return b, wfx

                stop_pe = None
                curd = down_ldwfx(0)
                for sl in range(16):
                    b, wfx = curd
                    curd = down_ldwfx(sl + 1) if sl + 1 < 16 else None
                    B.wait("tensor", ve, wfx)
                    if sl == 2:
                        flush_tails()
                    for c4 in range(4):
                        for tile in range(4):
                            lhs = hbuf[:, (sl * 4 + c4) * TB + tile * 128:
                                       (sl * 4 + c4) * TB + tile * 128 + 128]
                            for h in range(2):
                                rhs = wfsl[:, b * 4096 + c4 * 1024 + h * 512:
                                           b * 4096 + c4 * 1024 + h * 512 + 512]
                                st = (sl == 0 and c4 == 0)
                                sp = (sl == 15 and c4 == 3)
                                B.do("tensor", lambda e,
                                     o=P[tile][:, h * 512:(h + 1) * 512],
                                     l=lhs, r=rhs, st=st, sp=sp:
                                     e.matmul(o, l, r, start=st, stop=sp))
                    slot_pe = B.inc_last("tensor", pe, 1)
                    ffree[b] = (pe, slot_pe)
                    if sl == 15:
                        stop_pe = slot_pe

                for tile in range(4):
                    n = chain_idx[0]; chain_idx[0] += 1
                    gt = tb * 4 + tile
                    B.wait("vector", pe, stop_pe)
                    B.waitp("vector", sd_ld)
                    B.wait("vector", ve, ve.n)
                    ev1 = B.do("vector", lambda e, tile=tile, dh=dh: e.tensor_mul(
                        P[tile][:, 0:1024], P[tile][:, 0:1024],
                        sdbc[:, dh * 1024:(dh + 1) * 1024]), ve, 1)
                    B.wait("scalar", ve, ev1)
                    B.wait("scalar", sc, sc.n)
                    if n >= 4:
                        B.wait("scalar", ve, ch_ev3[n - 4])
                    ch_ev2[n] = B.do("scalar", lambda e, tile=tile, n=n: e.activation(
                        gfin[:, (n % 4) * 1024:(n % 4) * 1024 + 1024],
                        P[tile][:, 0:1024], AF.Silu), sc, 1)
                    pf8[2 * tile] = (sc, ch_ev2[n])
                    pf8[2 * tile + 1] = (sc, ch_ev2[n])
                    tails.append(lambda n=n, gt=gt, dh=dh: emit_tail(n, gt, dh))

        flush_tails()

        for s in SP.sems + GP.sems:
            B.wait("sync", s, s.n)

        with nc.Block() as block:
            B.install(block)

    return nc


def _host_prep(L0_attn_x, L0_attn_Wq, L0_attn_Wk, L0_attn_Wv, L0_attn_Wo,
               L0_ffn_gate_w, L0_ffn_gate_scale, L0_ffn_up_w, L0_ffn_up_scale,
               L0_ffn_down_w, L0_ffn_down_scale):
    x = np.asarray(L0_attn_x, dtype=np.float32)
    def tile_attn128(w):
        wt = np.asarray(w, dtype=np.float32).T.reshape(16, 128, 16, 128)
        return np.ascontiguousarray(wt.transpose(2, 1, 0, 3).reshape(16, 128, 2048))

    def tile_attn256(w):
        wt = np.asarray(w, dtype=np.float32).T.reshape(16, 128, 8, 256)
        return np.ascontiguousarray(wt.transpose(2, 1, 0, 3).reshape(8, 128, 4096))

    wqT = tile_attn128(L0_attn_Wq)
    wkT = tile_attn128(L0_attn_Wk)
    wvT = tile_attn128(L0_attn_Wv)
    woT = tile_attn256(L0_attn_Wo)

    def tile_gate(w):
        wt = np.asarray(w).astype(np.uint8).T.reshape(16, 128, 32, 256)
        return np.ascontiguousarray(wt.transpose(2, 1, 0, 3).reshape(32, 128, 4096))

    wgT = tile_gate(L0_ffn_gate_w)
    wuT = tile_gate(L0_ffn_up_w)
    wdt = np.asarray(L0_ffn_down_w).astype(np.uint8).T.reshape(16, 4, 128, 2, 1024)
    wdT = np.ascontiguousarray(wdt.transpose(3, 0, 2, 1, 4).reshape(2, 16, 128, 4096))
    sg = np.ascontiguousarray(np.asarray(L0_ffn_gate_scale, dtype=np.float32)
                              .reshape(NIC, 128).T)
    su = np.ascontiguousarray(np.asarray(L0_ffn_up_scale, dtype=np.float32)
                              .reshape(NIC, 128).T)
    sd = np.ascontiguousarray(np.broadcast_to(
        np.asarray(L0_ffn_down_scale, dtype=np.float32), (128, D)))

    in_maps = []
    for c in range(8):
        b, j = c // 4, c % 4
        xs = np.ascontiguousarray(x[b, j * T:(j + 1) * T, :])
        mvec = np.zeros(8, dtype=np.float32)
        for r in range(8):
            if r // 4 == b and r % 4 < j:
                mvec[r] = 1.0
        mask = np.ascontiguousarray(np.broadcast_to(mvec, (128, 8)))
        in_maps.append(dict(x=xs, wqT=wqT, wkT=wkT, wvT=wvT, woT=woT,
                            wgT=wgT, wuT=wuT, wdT=wdT, sg=sg, su=su, sd=sd,
                            mask=mask))
    return in_maps


def kernel(**inputs):
    in_maps = _host_prep(**inputs)
    nc = _build_nc()
    res = run_bass_kernel_spmd(nc, in_maps, core_ids=list(range(8)))
    if DEBUG:
        kernel.last_results = res
    full = np.empty((2, 8192, D), dtype=np.float32)
    for c in range(8):
        b, j = c // 4, c % 4
        full[b, j * T:(j + 1) * T, :] = np.asarray(res.results[c]["out"],
                                                   dtype=np.float32)
    return full


# version 14
# speedup vs baseline: 1.0024x; 1.0024x over previous
import numpy as np
from contextlib import ExitStack
import concourse.bass as bass
import concourse.mybir as mybir
from concourse.bass_utils import run_bass_kernel_spmd

F32 = mybir.dt.float32
BF16 = mybir.dt.bfloat16
U8 = mybir.dt.uint8
AF = mybir.ActivationFunctionType
ALU = mybir.AluOpType

D = 2048
II = 8192
T = 2048
NT = 16
NE = 16
ND = 16
NIC = 64
SBT = 1024
TB = 512
DENOM = float(np.sqrt(np.float32(2048.0)) / np.float32(8192.0))
EPS = 1e-5

DEBUG = False
_NO_CC = False
_PHASE = 5


class Sem:
    __slots__ = ("h", "n")

    def __init__(self, h):
        self.h = h
        self.n = 0


class Build:
    ENGS = ("sync", "gpsimd", "scalar", "vector", "tensor")

    def __init__(self):
        self.ops = {e: [] for e in self.ENGS}
        self._waited = {e: {} for e in self.ENGS}

    def wait(self, eng, sem, val):
        if val is None or val <= 0:
            return
        w = self._waited[eng]
        if w.get(id(sem), -1) >= val:
            return
        w[id(sem)] = val
        self.ops[eng].append(lambda e, h=sem.h, v=val: e.wait_ge(h, v))

    def waitp(self, eng, pair):
        if pair is not None:
            self.wait(eng, pair[0], pair[1])

    def do(self, eng, fn, sem=None, k=1):
        if sem is None:
            self.ops[eng].append(fn)
            return None
        sem.n += k
        tgt = sem.n
        self.ops[eng].append(lambda e, f=fn, h=sem.h, kk=k: f(e).then_inc(h, kk))
        return tgt

    def inc_last(self, eng, sem, k=1):
        sem.n += k
        tgt = sem.n
        prev = self.ops[eng].pop()
        self.ops[eng].append(lambda e, p=prev, h=sem.h, kk=k: p(e).then_inc(h, kk))
        return tgt

    def barrier(self, sems):
        snap = [(s, s.n) for s in sems]
        for eng in self.ENGS:
            for s, v in snap:
                self.wait(eng, s, v)

    def install(self, block):
        for name in self.ENGS:
            ops = self.ops[name]

            def body(e, ops=ops):
                for fn in ops:
                    fn(e)

            getattr(block, name)(body)


class DmaPool:

    def __init__(self, B, sems):
        self.B = B
        self.sems = sems
        self.i = 0

    def issue(self, eng, fn):
        s = self.sems[self.i % len(self.sems)]
        self.i += 1
        self.B.wait(eng, s, s.n)
        tgt = self.B.do(eng, fn, s, 16)
        return (s, tgt)


def _build_nc():
    nc = bass.Bass("TRN2", target_bir_lowering=False, detect_race_conditions=False)

    d_x = nc.dram_tensor("x", [T, D], F32, kind="ExternalInput").ap()
    d_wqT = nc.dram_tensor("wqT", [16, 128, 2048], F32, kind="ExternalInput").ap()
    d_wkT = nc.dram_tensor("wkT", [16, 128, 2048], F32, kind="ExternalInput").ap()
    d_wvT = nc.dram_tensor("wvT", [16, 128, 2048], F32, kind="ExternalInput").ap()
    d_woT = nc.dram_tensor("woT", [8, 128, 4096], F32, kind="ExternalInput").ap()
    d_wgT = nc.dram_tensor("wgT", [32, 128, 4096], U8, kind="ExternalInput").ap()
    d_wuT = nc.dram_tensor("wuT", [32, 128, 4096], U8, kind="ExternalInput").ap()
    d_wdT = nc.dram_tensor("wdT", [2, 16, 128, 4096], U8, kind="ExternalInput").ap()
    d_sg = nc.dram_tensor("sg", [128, NIC], F32, kind="ExternalInput").ap()
    d_su = nc.dram_tensor("su", [128, NIC], F32, kind="ExternalInput").ap()
    d_sd = nc.dram_tensor("sd", [128, D], F32, kind="ExternalInput").ap()
    d_mask = nc.dram_tensor("mask", [128, 8], F32, kind="ExternalInput").ap()
    d_out = nc.dram_tensor("out", [T, D], F32, kind="ExternalOutput").ap()

    s_kind = "ExternalOutput" if DEBUG else "Internal"
    d_sscr = nc.dram_tensor("s_scr", [T, D], BF16, kind=s_kind).ap()
    t_ccin = nc.dram_tensor("cc_in", [128, NE], F32)
    t_ccout = nc.dram_tensor("cc_out", [128 * 8, NE], F32)
    if DEBUG:
        d_dbg_xnT = nc.dram_tensor("dbg_xnT", [128, NE * T], BF16, kind="ExternalOutput").ap()
        d_dbg_kv = nc.dram_tensor("dbg_kv", [128, NE * T], BF16, kind="ExternalOutput").ap()
        d_dbg_carry = nc.dram_tensor("dbg_carry", [128, NE], F32, kind="ExternalOutput").ap()
        d_dbg_tot = nc.dram_tensor("dbg_tot", [128, NE], F32, kind="ExternalOutput").ap()
        d_dbg_q0 = nc.dram_tensor("dbg_q0", [128, NE * SBT], BF16, kind="ExternalOutput").ap()

    B = Build()
    es = ExitStack()
    with es:
        E = es.enter_context
        pe = Sem(E(nc.semaphore("pe")))
        sc = Sem(E(nc.semaphore("sc")))
        ve = Sem(E(nc.semaphore("ve")))
        gp = Sem(E(nc.semaphore("gp")))
        cc = Sem(E(nc.semaphore("cc")))
        SP = DmaPool(B, [Sem(E(nc.semaphore(f"dp{i}"))) for i in range(12)])
        GP = DmaPool(B, [Sem(E(nc.semaphore(f"dq{i}"))) for i in range(12)])
        allsems = [pe, sc, ve, gp] + SP.sems + GP.sems

        ident = E(nc.sbuf_tensor("ident", [128, 128], BF16))
        zeros = E(nc.sbuf_tensor("zeros", [128, SBT], BF16))
        sgb = E(nc.sbuf_tensor("sgb", [128, NIC], F32))
        sub = E(nc.sbuf_tensor("sub", [128, NIC], F32))
        maskb = E(nc.sbuf_tensor("maskb", [128, 8], F32))
        tot0 = E(nc.sbuf_tensor("tot0", [128, NE], F32))
        tot1 = E(nc.sbuf_tensor("tot1", [128, NE], F32))
        totA = E(nc.sbuf_tensor("totA", [128, NE], F32))
        gath = E(nc.sbuf_tensor("gath", [128, 8 * NE], F32))
        carry = E(nc.sbuf_tensor("carry", [128, NE], F32))
        carry2 = E(nc.sbuf_tensor("carry2", [128, NE], F32))
        stats = E(nc.sbuf_tensor("stats", [128, 16], F32))
        xnT = E(nc.sbuf_tensor("xnT", [128, NE * T], BF16))

        AB = E(nc.sbuf_tensor("AB", [128, 59392], BF16))
        AFa = E(nc.sbuf_tensor("AF", [128, 5120], F32))
        P = [E(nc.psum_tensor(f"P{i}", [128, 1024], F32)) for i in range(4)]


        zeros_rdy = B.do("gpsimd", lambda e: e.memset(zeros[:], 0.0), gp, 1)
        ms2 = B.do("gpsimd", lambda e: e.memset(ident[:], 0.0), gp, 1)
        B.wait("gpsimd", gp, ms2)
        ident_rdy = B.do("gpsimd", lambda e: e.affine_select(
            out=ident[:], in_=ident[:], compare_op=ALU.not_equal,
            fill=1.0, base=0, pattern=[[-1, 128]], channel_multiplier=1), gp, 1)

        xln = AFa[:, 0:4096]
        sqscr = AB[:, 0:2048]

        x_ld = [None] * NT
        ln_vA = [None] * NT
        ln_vC = [None] * NT
        ln_scD = [None] * NT
        tr_pe = [[None, None] for _ in range(NT)]
        tr_ve = [[None, None] for _ in range(NT)]

        xnT_r = xnT[:, :].rearrange("p (c n) -> p c n", n=T)

        kv = AB[:, 0:32768]
        qv = AB[:, 32768:49152]
        wsl = AB[:, 49152:57344]
        scol = AB[:, 57344:59392]
        ksb = AB[:, 32768:34816]
        kvt = AFa[:, 2048:3072]
        pf8 = [None] * 8
        wfree = [None, None]
        wqfree = [None, None, None, None]
        ring = [0]
        ring8 = [0]

        def pwait_full(eng, r):
            B.waitp(eng, pf8[2 * r])
            B.waitp(eng, pf8[2 * r + 1])

        def pset_full(r, pair):
            pf8[2 * r] = pair
            pf8[2 * r + 1] = pair
        stt_tgt = {}
        kvseq = [0]
        kvA = [0]
        xnT_sb0_ve = [None]

        def load_wslice(b, dram, sl, pe_free):
            B.waitp("gpsimd", pe_free)
            src = dram[sl, :, :]
            dst = wsl[:, b * 4096:(b + 1) * 4096]
            return GP.issue("gpsimd", lambda e, s=src, d=dst: e.dma_start(out=d, in_=s))

        def load_wslot(slot, dram, sl, pe_free):
            B.waitp("gpsimd", pe_free)
            src = dram[sl, :, :]
            dst = wsl[:, slot * 2048:(slot + 1) * 2048]
            return GP.issue("gpsimd", lambda e, s=src, d=dst: e.dma_start(out=d, in_=s))

        def emit_kv_chunk(ec, sb, r0, r1):
            seq = kvseq[0]; kvseq[0] += 1
            kslot = seq % 2
            vslot = 2 + seq % 2
            wk_rdy = load_wslot(kslot, d_wkT, ec, wqfree[kslot])
            wv_rdy = load_wslot(vslot, d_wvT, ec, wqfree[vslot])
            koff = ec * T + sb * SBT
            B.waitp("tensor", wk_rdy)
            B.waitp("tensor", wv_rdy)
            pwait_full("tensor", r0)
            if sb == 0:
                B.waitp("tensor", xnT_sb0_ve[0])
            for dc in range(ND):
                lhs = wsl[:, kslot * 2048 + dc * 128: kslot * 2048 + dc * 128 + 128]
                for h in range(2):
                    rhs = xnT[:, dc * T + sb * SBT + h * 512:
                              dc * T + sb * SBT + h * 512 + 512]
                    B.do("tensor", lambda e, o=P[r0][:, h * 512:(h + 1) * 512],
                         l=lhs, r=rhs, st=(dc == 0), sp=(dc == ND - 1):
                         e.matmul(o, l, r, start=st, stop=sp))
            k_pe = B.inc_last("tensor", pe, 1)
            pwait_full("tensor", r1)
            for dc in range(ND):
                lhs = wsl[:, vslot * 2048 + dc * 128: vslot * 2048 + dc * 128 + 128]
                for h in range(2):
                    rhs = xnT[:, dc * T + sb * SBT + h * 512:
                              dc * T + sb * SBT + h * 512 + 512]
                    B.do("tensor", lambda e, o=P[r1][:, h * 512:(h + 1) * 512],
                         l=lhs, r=rhs, st=(dc == 0), sp=(dc == ND - 1):
                         e.matmul(o, l, r, start=st, stop=sp))
            v_pe = B.inc_last("tensor", pe, 1)
            wqfree[kslot] = (pe, v_pe)
            wqfree[vslot] = (pe, v_pe)

            kview = ksb[:, (seq % 2) * 1024:(seq % 2) * 1024 + 1024]
            B.wait("scalar", pe, k_pe)
            B.wait("scalar", sc, sc.n)
            if seq >= 2:
                B.wait("scalar", ve, stt_tgt[seq - 2])
            kev = B.do("scalar", lambda e, kv_=kview, r0=r0:
                       e.copy(kv_, P[r0][:, :]), sc, 1)
            pset_full(r0, (sc, kev))

            B.wait("vector", pe, v_pe)
            B.wait("vector", sc, kev)
            B.wait("vector", ve, ve.n)
            tt = tot0 if sb == 0 else tot1
            stt = B.do("vector", lambda e, kv_=kview, r1=r1, koff=koff,
                       tt=tt, ec=ec: e.scalar_tensor_tensor(
                           kv[:, koff:koff + SBT], kv_, 1.0, P[r1][:, :],
                           ALU.mult, ALU.mult,
                           accum_out=tt[:, ec:ec + 1]), ve, 1)
            pset_full(r1, (ve, stt))
            stt_tgt[seq] = stt

        def vstep(fn):
            B.wait("vector", ve, ve.n)
            return B.do("vector", fn, ve, 1)

        def ln_views(t):
            xb = xln[:, (t % 2) * D:(t % 2) * D + D]
            xnbf = AB[:, 2048 + (t % 2) * 2048: 2048 + (t % 2) * 2048 + 2048]
            sb8 = (t % 2) * 8
            st = [stats[:, sb8 + i:sb8 + i + 1] for i in range(8)]
            return xb, xnbf, st

        def ln_stage1(t):
            xb, xnbf, st = ln_views(t)
            s2 = st[1]
            if t >= 2:
                B.wait("sync", sc, ln_scD[t - 2])
            x_ld[t] = SP.issue("sync", lambda e, t=t, xb=xb: e.dma_start(
                out=xb, in_=d_x[t * 128:(t + 1) * 128, :]))
            B.waitp("scalar", x_ld[t])
            B.wait("scalar", sc, sc.n)
            if t >= 2:
                B.wait("scalar", ve, ln_vA[t - 2])
            ln_sq2[t] = B.do("scalar", lambda e, xb=xb, s2=s2: e.activation(
                sqscr, xb, AF.Square, accum_out=s2), sc, 1)

        def ln_stageA(t):
            xb, xnbf, st = ln_views(t)
            s1, s2, mcol, t2c, varc = st[0], st[1], st[2], st[3], st[4]
            B.waitp("vector", x_ld[t])
            if t >= 2:
                B.wait("vector", sc, ln_scD[t - 2])
            vstep(lambda e, xb=xb, s1=s1: e.reduce_sum(s1, xb, axis=mybir.AxisListType.X))
            vstep(lambda e, mcol=mcol, s1=s1: e.tensor_scalar_mul(mcol, s1, 1.0 / D))
            B.wait("vector", sc, ln_sq2[t])
            vstep(lambda e, t2c=t2c, s2=s2: e.tensor_scalar_mul(t2c, s2, 1.0 / D))
            vstep(lambda e, varc=varc, mcol=mcol: e.tensor_mul(varc, mcol, mcol))
            vstep(lambda e, varc=varc, t2c=t2c: e.tensor_sub(varc, t2c, varc))
            ln_vA[t] = vstep(lambda e, varc=varc: e.tensor_scalar_add(varc, varc, EPS))
            B.wait("scalar", ve, ln_vA[t])
            B.wait("scalar", sc, sc.n)
            ln_sB[t] = B.do("scalar", lambda e, sdc=st[5], varc=varc:
                            e.sqrt(sdc, varc), sc, 1)

        def ln_stageB(t):
            xb, xnbf, st = ln_views(t)
            mcol, sdc, rstd, nbc = st[2], st[5], st[6], st[7]
            B.wait("vector", sc, ln_sB[t])
            vstep(lambda e, rstd=rstd, sdc=sdc: e.reciprocal(rstd, sdc))
            ln_vC[t] = vstep(lambda e, nbc=nbc, mcol=mcol, rstd=rstd: e.scalar_tensor_tensor(
                nbc, mcol, -1.0, rstd, ALU.mult, ALU.mult))

            B.wait("scalar", ve, ln_vC[t])
            B.wait("scalar", sc, sc.n)
            if t >= 2:
                B.wait("scalar", pe, tr_pe[t - 2][1])
            ln_scD[t] = B.do("scalar", lambda e, xb=xb, xnbf=xnbf, nbc=nbc,
                             rstd=rstd: e.activation(
                xnbf, xb, AF.Identity, bias=nbc, scale=rstd), sc, 1)

            B.wait("tensor", sc, ln_scD[t])
            B.wait("tensor", gp, ident_rdy)
            for g in range(2):
                if t >= 1:
                    B.waitp("tensor", tr_ve[t - 1][g])
                for j in range(8):
                    c = g * 8 + j
                    B.do("tensor", lambda e, g=g, j=j, c=c, xnbf=xnbf: e.matmul(
                        P[g][:, j * 128:(j + 1) * 128],
                        xnbf[:, c * 128:(c + 1) * 128], ident[:],
                        start=True, stop=True))
                tr_pe[t][g] = B.inc_last("tensor", pe, 1)

                out_ap = xnT_r[:, g * 8:(g + 1) * 8, t * 128:(t + 1) * 128]
                in_ap = P[g][:, 0:1024].rearrange("p (j n) -> p j n", n=128)
                B.wait("vector", pe, tr_pe[t][g])
                B.wait("vector", ve, ve.n)
                tr_ve[t][g] = (ve, B.do("vector", lambda e, o=out_ap, i=in_ap:
                                        e.tensor_copy(o, i), ve, 1))

            if t == 7:
                xnT_sb0_ve[0] = tr_ve[7][1]
            if t >= 8 and _PHASE >= 2:
                emit_kv_chunk(8 + kvA[0], 0, 2, 3)
                kvA[0] += 1

        ln_sq2 = [None] * NT
        ln_sB = [None] * NT
        ln_stage1(0)
        ln_stage1(1)
        ln_stageA(0)
        for t in range(NT):
            ln_stageB(t)
            if t + 2 < NT:
                ln_stage1(t + 2)
            if t + 1 < NT:
                ln_stageA(t + 1)
        sg_ld = SP.issue("sync", lambda e: e.dma_start(out=sgb[:], in_=d_sg))
        su_ld = SP.issue("sync", lambda e: e.dma_start(out=sub[:], in_=d_su))
        mask_ld = SP.issue("sync", lambda e: e.dma_start(out=maskb[:], in_=d_mask))

        if _PHASE < 2:
            for s_ in SP.sems + GP.sems:
                B.wait("sync", s_, s_.n)
            with nc.Block() as block:
                B.install(block)
            return nc

        for ec in range(8):
            emit_kv_chunk(ec, 0, 2, 3)

        pset_full(0, tr_ve[NT - 1][0])
        pset_full(1, tr_ve[NT - 1][1])
        for ec in range(NE):
            r0 = ring[0] % 4; ring[0] += 1
            r1 = ring[0] % 4; ring[0] += 1
            emit_kv_chunk(ec, 1, r0, r1)
        wfree[0] = (pe, pe.n)
        wfree[1] = (pe, pe.n)

        if _PHASE < 3:
            for s_ in SP.sems + GP.sems:
                B.wait("sync", s_, s_.n)
            B.barrier(allsems)
            with nc.Block() as block:
                B.install(block)
            return nc

        B.wait("vector", ve, ve.n)
        totA_ve = B.do("vector", lambda e: e.tensor_add(totA[:], tot0[:], tot1[:]), ve, 1)
        B.wait("sync", ve, totA_ve)
        ccw = SP.issue("sync", lambda e: e.dma_start(out=t_ccin.ap(), in_=totA[:]))
        B.waitp("gpsimd", ccw)
        if not _NO_CC:
            B.do("gpsimd", lambda e: e.collective_compute(
                "AllGather", ALU.bypass, replica_groups=[list(range(8))],
                ins=[t_ccin.ap().opt()], outs=[t_ccout.ap().opt()]), cc, 1)
        carry_ve = None

        if _PHASE < 4:
            for s_ in SP.sems + GP.sems:
                B.wait("sync", s_, s_.n)
            B.barrier(allsems)
            with nc.Block() as block:
                B.install(block)
            return nc

        s_wr = [None] * 16
        wo_done_pe = None
        for sb in range(2):
            wqfree[0] = wqfree[1] = wfree[0]
            wqfree[2] = wqfree[3] = wfree[1]
            qv_sc = [None] * NE
            for sl in range(NE):
                slot = sl % 4
                w_rdy = load_wslot(slot, d_wqT, sl, wqfree[slot])
                ec = sl
                r0 = ring[0] % 4; ring[0] += 1
                B.waitp("tensor", w_rdy)
                pwait_full("tensor", r0)
                for dc in range(ND):
                    lhs = wsl[:, slot * 2048 + dc * 128: slot * 2048 + dc * 128 + 128]
                    for h in range(2):
                        rhs = xnT[:, dc * T + sb * SBT + h * 512:
                                  dc * T + sb * SBT + h * 512 + 512]
                        B.do("tensor", lambda e, o=P[r0][:, h * 512:(h + 1) * 512],
                             l=lhs, r=rhs, st=(dc == 0), sp=(dc == ND - 1):
                             e.matmul(o, l, r, start=st, stop=sp))
                q_pe = B.inc_last("tensor", pe, 1)
                wqfree[slot] = (pe, q_pe)
                B.wait("scalar", pe, q_pe)
                B.wait("scalar", sc, sc.n)
                if ec < 2:
                    B.wait("scalar", ve, max(stt_tgt.values()))
                if sb == 1:
                    B.wait("scalar", pe, wo_done_pe)
                qv_sc[ec] = B.do("scalar", lambda e, ec=ec, r0=r0: e.activation(
                    qv[:, ec * SBT:(ec + 1) * SBT], P[r0][:, :],
                    AF.Identity, scale=DENOM), sc, 1)
                pset_full(r0, (sc, qv_sc[ec]))
            wfree[0] = (pe, pe.n)
            wfree[1] = (pe, pe.n)

            if DEBUG and sb == 0:
                B.wait("sync", sc, qv_sc[NE - 1])
                SP.issue("sync", lambda e: e.dma_start(out=d_dbg_q0, in_=qv))

            if sb == 0:
                B.wait("sync", cc, 1)
                g_lds = []
                for r in range(8):
                    g_lds.append(SP.issue("sync", lambda e, r=r: e.dma_start(
                        out=gath[:, r * NE:(r + 1) * NE],
                        in_=t_ccout.ap()[r * 128:(r + 1) * 128, :])))
                for t_ in g_lds:
                    B.waitp("vector", t_)
                B.waitp("vector", mask_ld)
                B.wait("vector", ve, ve.n)
                B.do("vector", lambda e: e.tensor_scalar(
                    carry[:], gath[:, 0:NE], maskb[:, 0:1], None, ALU.mult), ve, 1)
                for r in range(1, 8):
                    B.wait("vector", ve, ve.n)
                    carry_ve = B.do("vector", lambda e, r=r: e.scalar_tensor_tensor(
                        carry[:], gath[:, r * NE:(r + 1) * NE], maskb[:, r:r + 1],
                        carry[:], ALU.mult, ALU.add), ve, 1)

            vhat_ve = None
            B.wait("vector", gp, zeros_rdy)
            B.wait("vector", ve, max(stt_tgt.values()))
            if carry_ve:
                B.wait("vector", ve, carry_ve)
            for ec in range(NE):
                koff = ec * T + sb * SBT
                init = carry[:, ec:ec + 1] if sb == 0 else carry2[:, ec:ec + 1]
                B.wait("vector", sc, qv_sc[ec])
                B.wait("vector", ve, ve.n)
                B.do("vector", lambda e, koff=koff, init=init: e.tensor_tensor_scan(
                    kvt, kv[:, koff:koff + SBT], zeros[:, :], init,
                    ALU.add, ALU.add), ve, 1)
                if sb == 0:
                    B.wait("vector", ve, ve.n)
                    B.do("vector", lambda e, ec=ec: e.tensor_copy(
                        carry2[:, ec:ec + 1], kvt[:, SBT - 1:SBT]), ve, 1)
                B.wait("vector", ve, ve.n)
                vhat_ve = B.do("vector", lambda e, ec=ec: e.tensor_mul(
                    qv[:, ec * SBT:(ec + 1) * SBT],
                    qv[:, ec * SBT:(ec + 1) * SBT], kvt), ve, 1)

            B.wait("tensor", ve, vhat_ve)
            for sl in range(8):
                w_rdy = load_wslice(sl % 2, d_woT, sl, wfree[sl % 2])
                sc_evs = None
                gs = sb * 8 + sl
                for tile in range(8):
                    r8 = ring8[0] % 8; ring8[0] += 1
                    pview = P[r8 // 2][:, (r8 % 2) * 512:(r8 % 2) * 512 + 256]
                    B.waitp("tensor", w_rdy)
                    B.waitp("tensor", pf8[r8])
                    for ec in range(NE):
                        lhs = qv[:, ec * SBT + tile * 128: ec * SBT + tile * 128 + 128]
                        rhs = wsl[:, (sl % 2) * 4096 + ec * 256: (sl % 2) * 4096 + ec * 256 + 256]
                        B.do("tensor", lambda e, o=pview, l=lhs, r=rhs,
                             st=(ec == 0), sp=(ec == NE - 1):
                             e.matmul(o, l, r, start=st, stop=sp))
                    wo_pe = B.inc_last("tensor", pe, 1)
                    B.wait("scalar", pe, wo_pe)
                    B.wait("scalar", sc, sc.n)
                    if gs >= 1 and tile == 0:
                        B.waitp("scalar", s_wr[gs - 1])
                    sc_evs = B.do("scalar", lambda e, tile=tile, pview=pview:
                                  e.copy(scol[:, tile * 256: tile * 256 + 256],
                                         pview), sc, 1)
                    pf8[r8] = (sc, sc_evs)
                wfree[sl % 2] = (pe, pe.n)
                B.wait("sync", sc, sc_evs)
                dst = d_sscr[sb * SBT:(sb + 1) * SBT, sl * 256:(sl + 1) * 256] \
                    .rearrange("(tl p) f -> p tl f", p=128)
                src = scol.rearrange("p (tl f) -> p tl f", f=256)
                s_wr[gs] = SP.issue("sync", lambda e, d=dst, s=src:
                                    e.dma_start(out=d, in_=s))
            wo_done_pe = pe.n

        if DEBUG:
            B.wait("sync", ve, ve.n)
            B.wait("sync", sc, sc.n)
            SP.issue("sync", lambda e: e.dma_start(out=d_dbg_xnT, in_=xnT[:, :]))
            SP.issue("sync", lambda e: e.dma_start(out=d_dbg_kv, in_=kv))
            SP.issue("sync", lambda e: e.dma_start(out=d_dbg_carry, in_=carry[:, :]))
            SP.issue("sync", lambda e: e.dma_start(out=d_dbg_tot, in_=totA[:, :]))

        B.barrier(allsems)

        if _PHASE < 5:
            for s_ in SP.sems + GP.sems:
                B.wait("sync", s_, s_.n)
            with nc.Block() as block:
                B.install(block)
            return nc

        hbuf = AB[:, 0:32768]
        wfsl = AB[:, 32768:49152]
        shalf = AB[:, 49152:51200]
        gfin = AB[:, 51200:55296]
        mbuf = AB[:, 55296:57344]
        sdbc = AB[:, 57344:59392]
        xhalf = AFa[:, 0:2048]
        ohalf = AFa[:, 2048:4096]
        t0b = AFa[:, 4096:5120]

        ffree = [None, None, None, None]
        fslot = [0]

        sd_ld = GP.issue("gpsimd", lambda e: e.dma_start(out=sdbc, in_=d_sd))

        chain_idx = [0]
        ch_ev2 = {}; ch_ev3 = {}; ch_ev4 = {}; ch_ev5 = {}
        ch_xld = {}; ch_sld = {}
        tails = []

        def flush_tails():
            for fn in tails:
                fn()
            tails.clear()

        def emit_tail(n, gt, dh):
            if n >= 2:
                B.wait("sync", ve, ch_ev4[n - 2])
                B.wait("sync", ve, ch_ev3[n - 2])
            ch_xld[n] = SP.issue("sync", lambda e, gt=gt, dh=dh, n=n: e.dma_start(
                out=xhalf[:, (n % 2) * 1024:(n % 2) * 1024 + 1024],
                in_=d_x[gt * 128:(gt + 1) * 128, dh * 1024:(dh + 1) * 1024]))
            ch_sld[n] = SP.issue("sync", lambda e, gt=gt, dh=dh, n=n: e.dma_start(
                out=shalf[:, (n % 2) * 1024:(n % 2) * 1024 + 1024],
                in_=d_sscr[gt * 128:(gt + 1) * 128, dh * 1024:(dh + 1) * 1024]))
            B.wait("vector", sc, ch_ev2[n])
            B.waitp("vector", ch_sld[n])
            B.wait("vector", ve, ve.n)
            ch_ev3[n] = B.do("vector", lambda e, n=n: e.tensor_mul(
                mbuf[:, (n % 2) * 1024:(n % 2) * 1024 + 1024],
                gfin[:, (n % 4) * 1024:(n % 4) * 1024 + 1024],
                shalf[:, (n % 2) * 1024:(n % 2) * 1024 + 1024]), ve, 1)
            B.waitp("vector", ch_xld[n])
            if n >= 2:
                B.waitp("vector", ch_ev5[n - 2])
            B.wait("vector", ve, ve.n)
            ch_ev4[n] = B.do("vector", lambda e, n=n: e.tensor_add(
                ohalf[:, (n % 2) * 1024:(n % 2) * 1024 + 1024],
                mbuf[:, (n % 2) * 1024:(n % 2) * 1024 + 1024],
                xhalf[:, (n % 2) * 1024:(n % 2) * 1024 + 1024]), ve, 1)
            B.wait("sync", ve, ch_ev4[n])
            ch_ev5[n] = SP.issue("sync", lambda e, gt=gt, dh=dh, n=n: e.dma_start(
                out=d_out[gt * 128:(gt + 1) * 128, dh * 1024:(dh + 1) * 1024],
                in_=ohalf[:, (n % 2) * 1024:(n % 2) * 1024 + 1024]))

        for tb in range(4):
            up_last_ve = None
            hev_gate = [None] * NIC
            for mat in range(2):
                dram = d_wgT if mat == 0 else d_wuT
                scl = sgb if mat == 0 else sub

                def gate_ldwfx(sl, dram=dram):
                    b = fslot[0] % 4; fslot[0] += 1
                    B.waitp("gpsimd", ffree[b])
                    src = dram[sl, :, :]
                    dst = wfsl[:, b * 4096:(b + 1) * 4096]
                    wld = GP.issue("gpsimd", lambda e, s=src, d=dst:
                                   e.dma_start(out=d, in_=s))
                    B.waitp("vector", wld)
                    B.waitp("vector", ffree[b])
                    B.wait("vector", ve, ve.n)
                    wfx = B.do("vector", lambda e, b=b: e.tensor_scalar_add(
                        wfsl[:, b * 4096:(b + 1) * 4096],
                        wfsl[:, b * 4096:(b + 1) * 4096], -128.0), ve, 1)
                    return b, wfx

                cur = gate_ldwfx(0)
                for sl in range(32):
                    b, wfx = cur
                    cur = gate_ldwfx(sl + 1) if sl + 1 < 32 else None
                    if mat == 0 and sl == 2:
                        flush_tails()
                    last_mm = None
                    for j in range(2):
                        ic = sl * 2 + j
                        r8 = ring8[0] % 8; ring8[0] += 1
                        pview = P[r8 // 2][:, (r8 % 2) * 512:(r8 % 2) * 512 + TB]
                        B.wait("tensor", ve, wfx)
                        B.waitp("tensor", pf8[r8])
                        for dc in range(ND):
                            lhs = wfsl[:, b * 4096 + dc * 256 + j * 128:
                                       b * 4096 + dc * 256 + j * 128 + 128]
                            rhs = xnT[:, dc * T + tb * TB: dc * T + tb * TB + TB]
                            B.do("tensor", lambda e, o=pview, l=lhs, r=rhs,
                                 st=(dc == 0), sp=(dc == ND - 1):
                                 e.matmul(o, l, r, start=st, stop=sp))
                        g_pe = B.inc_last("tensor", pe, 1)
                        last_mm = g_pe
                        if mat == 0:
                            B.wait("scalar", pe, g_pe)
                            B.wait("scalar", sc, sc.n)
                            hev = B.do("scalar", lambda e, ic=ic, pview=pview, scl=scl:
                                       e.activation(hbuf[:, ic * TB:(ic + 1) * TB],
                                                    pview, AF.Silu,
                                                    scale=scl[:, ic:ic + 1]), sc, 1)
                            pf8[r8] = (sc, hev)
                            hev_gate[ic] = hev
                        else:
                            B.wait("vector", pe, g_pe)
                            B.wait("vector", sc, hev_gate[ic])
                            B.wait("vector", ve, ve.n)
                            hev = B.do("vector", lambda e, ic=ic, pview=pview, scl=scl:
                                       e.scalar_tensor_tensor(
                                           hbuf[:, ic * TB:(ic + 1) * TB],
                                           pview, scl[:, ic:ic + 1],
                                           hbuf[:, ic * TB:(ic + 1) * TB],
                                           ALU.mult, ALU.mult), ve, 1)
                            pf8[r8] = (ve, hev)
                            up_last_ve = hev
                    ffree[b] = (pe, last_mm)

            B.wait("tensor", ve, up_last_ve)
            for dh in range(2):
                for r8_ in range(8):
                    B.waitp("tensor", pf8[r8_])

                def down_ldwfx(sl, dh=dh):
                    b = fslot[0] % 4; fslot[0] += 1
                    B.waitp("gpsimd", ffree[b])
                    src = d_wdT[dh, sl, :, :]
                    dst = wfsl[:, b * 4096:(b + 1) * 4096]
                    wld = GP.issue("gpsimd", lambda e, s=src, d=dst:
                                   e.dma_start(out=d, in_=s))
                    B.waitp("vector", wld)
                    B.waitp("vector", ffree[b])
                    B.wait("vector", ve, ve.n)
                    wfx = B.do("vector", lambda e, b=b: e.tensor_scalar_add(
                        wfsl[:, b * 4096:(b + 1) * 4096],
                        wfsl[:, b * 4096:(b + 1) * 4096], -128.0), ve, 1)
                    return b, wfx

                stop_pe = None
                curd = down_ldwfx(0)
                for sl in range(16):
                    b, wfx = curd
                    curd = down_ldwfx(sl + 1) if sl + 1 < 16 else None
                    B.wait("tensor", ve, wfx)
                    if sl == 2:
                        flush_tails()
                    for c4 in range(4):
                        for tile in range(4):
                            lhs = hbuf[:, (sl * 4 + c4) * TB + tile * 128:
                                       (sl * 4 + c4) * TB + tile * 128 + 128]
                            for h in range(2):
                                rhs = wfsl[:, b * 4096 + c4 * 1024 + h * 512:
                                           b * 4096 + c4 * 1024 + h * 512 + 512]
                                st = (sl == 0 and c4 == 0)
                                sp = (sl == 15 and c4 == 3)
                                B.do("tensor", lambda e,
                                     o=P[tile][:, h * 512:(h + 1) * 512],
                                     l=lhs, r=rhs, st=st, sp=sp:
                                     e.matmul(o, l, r, start=st, stop=sp))
                    slot_pe = B.inc_last("tensor", pe, 1)
                    ffree[b] = (pe, slot_pe)
                    if sl == 15:
                        stop_pe = slot_pe

                for tile in range(4):
                    n = chain_idx[0]; chain_idx[0] += 1
                    gt = tb * 4 + tile
                    B.wait("vector", pe, stop_pe)
                    B.waitp("vector", sd_ld)
                    B.wait("vector", ve, ve.n)
                    ev1 = B.do("vector", lambda e, tile=tile, dh=dh: e.tensor_mul(
                        P[tile][:, 0:1024], P[tile][:, 0:1024],
                        sdbc[:, dh * 1024:(dh + 1) * 1024]), ve, 1)
                    B.wait("scalar", ve, ev1)
                    B.wait("scalar", sc, sc.n)
                    if n >= 4:
                        B.wait("scalar", ve, ch_ev3[n - 4])
                    ch_ev2[n] = B.do("scalar", lambda e, tile=tile, n=n: e.activation(
                        gfin[:, (n % 4) * 1024:(n % 4) * 1024 + 1024],
                        P[tile][:, 0:1024], AF.Silu), sc, 1)
                    pf8[2 * tile] = (sc, ch_ev2[n])
                    pf8[2 * tile + 1] = (sc, ch_ev2[n])
                    tails.append(lambda n=n, gt=gt, dh=dh: emit_tail(n, gt, dh))

        flush_tails()

        for s in SP.sems + GP.sems:
            B.wait("sync", s, s.n)

        with nc.Block() as block:
            B.install(block)

    return nc


def _host_prep(L0_attn_x, L0_attn_Wq, L0_attn_Wk, L0_attn_Wv, L0_attn_Wo,
               L0_ffn_gate_w, L0_ffn_gate_scale, L0_ffn_up_w, L0_ffn_up_scale,
               L0_ffn_down_w, L0_ffn_down_scale):
    x = np.asarray(L0_attn_x, dtype=np.float32)
    def tile_attn128(w):
        wt = np.asarray(w, dtype=np.float32).T.reshape(16, 128, 16, 128)
        return np.ascontiguousarray(wt.transpose(2, 1, 0, 3).reshape(16, 128, 2048))

    def tile_attn256(w):
        wt = np.asarray(w, dtype=np.float32).T.reshape(16, 128, 8, 256)
        return np.ascontiguousarray(wt.transpose(2, 1, 0, 3).reshape(8, 128, 4096))

    wqT = tile_attn128(L0_attn_Wq)
    wkT = tile_attn128(L0_attn_Wk)
    wvT = tile_attn128(L0_attn_Wv)
    woT = tile_attn256(L0_attn_Wo)

    def tile_gate(w):
        wt = np.asarray(w).astype(np.uint8).T.reshape(16, 128, 32, 256)
        return np.ascontiguousarray(wt.transpose(2, 1, 0, 3).reshape(32, 128, 4096))

    wgT = tile_gate(L0_ffn_gate_w)
    wuT = tile_gate(L0_ffn_up_w)
    wdt = np.asarray(L0_ffn_down_w).astype(np.uint8).T.reshape(16, 4, 128, 2, 1024)
    wdT = np.ascontiguousarray(wdt.transpose(3, 0, 2, 1, 4).reshape(2, 16, 128, 4096))
    sg = np.ascontiguousarray(np.asarray(L0_ffn_gate_scale, dtype=np.float32)
                              .reshape(NIC, 128).T)
    su = np.ascontiguousarray(np.asarray(L0_ffn_up_scale, dtype=np.float32)
                              .reshape(NIC, 128).T)
    sd = np.ascontiguousarray(np.broadcast_to(
        np.asarray(L0_ffn_down_scale, dtype=np.float32), (128, D)))

    in_maps = []
    for c in range(8):
        b, j = c // 4, c % 4
        xs = np.ascontiguousarray(x[b, j * T:(j + 1) * T, :])
        mvec = np.zeros(8, dtype=np.float32)
        for r in range(8):
            if r // 4 == b and r % 4 < j:
                mvec[r] = 1.0
        mask = np.ascontiguousarray(np.broadcast_to(mvec, (128, 8)))
        in_maps.append(dict(x=xs, wqT=wqT, wkT=wkT, wvT=wvT, woT=woT,
                            wgT=wgT, wuT=wuT, wdT=wdT, sg=sg, su=su, sd=sd,
                            mask=mask))
    return in_maps


def kernel(**inputs):
    in_maps = _host_prep(**inputs)
    nc = _build_nc()
    res = run_bass_kernel_spmd(nc, in_maps, core_ids=list(range(8)))
    if DEBUG:
        kernel.last_results = res
    full = np.empty((2, 8192, D), dtype=np.float32)
    for c in range(8):
        b, j = c // 4, c % 4
        full[b, j * T:(j + 1) * T, :] = np.asarray(res.results[c]["out"],
                                                   dtype=np.float32)
    return full


# version 15
# speedup vs baseline: 1.0035x; 1.0012x over previous
import numpy as np
from contextlib import ExitStack
import concourse.bass as bass
import concourse.mybir as mybir
from concourse.bass_utils import run_bass_kernel_spmd

F32 = mybir.dt.float32
BF16 = mybir.dt.bfloat16
U8 = mybir.dt.uint8
AF = mybir.ActivationFunctionType
ALU = mybir.AluOpType

D = 2048
II = 8192
T = 2048
NT = 16
NE = 16
ND = 16
NIC = 64
SBT = 1024
TB = 512
DENOM = float(np.sqrt(np.float32(2048.0)) / np.float32(8192.0))
EPS = 1e-5

DEBUG = False
_NO_CC = False
_PHASE = 5


class Sem:
    __slots__ = ("h", "n")

    def __init__(self, h):
        self.h = h
        self.n = 0


class Build:
    ENGS = ("sync", "gpsimd", "scalar", "vector", "tensor")

    def __init__(self):
        self.ops = {e: [] for e in self.ENGS}
        self._waited = {e: {} for e in self.ENGS}

    def wait(self, eng, sem, val):
        if val is None or val <= 0:
            return
        w = self._waited[eng]
        if w.get(id(sem), -1) >= val:
            return
        w[id(sem)] = val
        self.ops[eng].append(lambda e, h=sem.h, v=val: e.wait_ge(h, v))

    def waitp(self, eng, pair):
        if pair is not None:
            self.wait(eng, pair[0], pair[1])

    def do(self, eng, fn, sem=None, k=1):
        if sem is None:
            self.ops[eng].append(fn)
            return None
        sem.n += k
        tgt = sem.n
        self.ops[eng].append(lambda e, f=fn, h=sem.h, kk=k: f(e).then_inc(h, kk))
        return tgt

    def inc_last(self, eng, sem, k=1):
        sem.n += k
        tgt = sem.n
        prev = self.ops[eng].pop()
        self.ops[eng].append(lambda e, p=prev, h=sem.h, kk=k: p(e).then_inc(h, kk))
        return tgt

    def barrier(self, sems):
        snap = [(s, s.n) for s in sems]
        for eng in self.ENGS:
            for s, v in snap:
                self.wait(eng, s, v)

    def install(self, block):
        for name in self.ENGS:
            ops = self.ops[name]

            def body(e, ops=ops):
                for fn in ops:
                    fn(e)

            getattr(block, name)(body)


class DmaPool:

    def __init__(self, B, sems):
        self.B = B
        self.sems = sems
        self.i = 0

    def issue(self, eng, fn):
        s = self.sems[self.i % len(self.sems)]
        self.i += 1
        self.B.wait(eng, s, s.n)
        tgt = self.B.do(eng, fn, s, 16)
        return (s, tgt)


def _build_nc():
    nc = bass.Bass("TRN2", target_bir_lowering=False, detect_race_conditions=False)

    d_x = nc.dram_tensor("x", [T, D], F32, kind="ExternalInput").ap()
    d_wqT = nc.dram_tensor("wqT", [16, 128, 2048], F32, kind="ExternalInput").ap()
    d_wkT = nc.dram_tensor("wkT", [16, 128, 2048], F32, kind="ExternalInput").ap()
    d_wvT = nc.dram_tensor("wvT", [16, 128, 2048], F32, kind="ExternalInput").ap()
    d_woT = nc.dram_tensor("woT", [8, 128, 4096], F32, kind="ExternalInput").ap()
    d_wgT = nc.dram_tensor("wgT", [32, 128, 4096], U8, kind="ExternalInput").ap()
    d_wuT = nc.dram_tensor("wuT", [32, 128, 4096], U8, kind="ExternalInput").ap()
    d_wdT = nc.dram_tensor("wdT", [2, 16, 128, 4096], U8, kind="ExternalInput").ap()
    d_sg = nc.dram_tensor("sg", [128, NIC], F32, kind="ExternalInput").ap()
    d_su = nc.dram_tensor("su", [128, NIC], F32, kind="ExternalInput").ap()
    d_sd = nc.dram_tensor("sd", [128, D], F32, kind="ExternalInput").ap()
    d_mask = nc.dram_tensor("mask", [128, 8], F32, kind="ExternalInput").ap()
    d_out = nc.dram_tensor("out", [T, D], F32, kind="ExternalOutput").ap()

    s_kind = "ExternalOutput" if DEBUG else "Internal"
    d_sscr = nc.dram_tensor("s_scr", [T, D], BF16, kind=s_kind).ap()
    t_ccin = nc.dram_tensor("cc_in", [128, NE], F32)
    t_ccout = nc.dram_tensor("cc_out", [128 * 8, NE], F32)
    if DEBUG:
        d_dbg_xnT = nc.dram_tensor("dbg_xnT", [128, NE * T], BF16, kind="ExternalOutput").ap()
        d_dbg_kv = nc.dram_tensor("dbg_kv", [128, NE * T], BF16, kind="ExternalOutput").ap()
        d_dbg_carry = nc.dram_tensor("dbg_carry", [128, NE], F32, kind="ExternalOutput").ap()
        d_dbg_tot = nc.dram_tensor("dbg_tot", [128, NE], F32, kind="ExternalOutput").ap()
        d_dbg_q0 = nc.dram_tensor("dbg_q0", [128, NE * SBT], BF16, kind="ExternalOutput").ap()

    B = Build()
    es = ExitStack()
    with es:
        E = es.enter_context
        pe = Sem(E(nc.semaphore("pe")))
        sc = Sem(E(nc.semaphore("sc")))
        ve = Sem(E(nc.semaphore("ve")))
        gp = Sem(E(nc.semaphore("gp")))
        cc = Sem(E(nc.semaphore("cc")))
        SP = DmaPool(B, [Sem(E(nc.semaphore(f"dp{i}"))) for i in range(12)])
        GP = DmaPool(B, [Sem(E(nc.semaphore(f"dq{i}"))) for i in range(12)])
        allsems = [pe, sc, ve, gp] + SP.sems + GP.sems

        ident = E(nc.sbuf_tensor("ident", [128, 128], BF16))
        zeros = E(nc.sbuf_tensor("zeros", [128, SBT], BF16))
        sgb = E(nc.sbuf_tensor("sgb", [128, NIC], F32))
        sub = E(nc.sbuf_tensor("sub", [128, NIC], F32))
        maskb = E(nc.sbuf_tensor("maskb", [128, 8], F32))
        tot0 = E(nc.sbuf_tensor("tot0", [128, NE], F32))
        tot1 = E(nc.sbuf_tensor("tot1", [128, NE], F32))
        totA = E(nc.sbuf_tensor("totA", [128, NE], F32))
        gath = E(nc.sbuf_tensor("gath", [128, 8 * NE], F32))
        carry = E(nc.sbuf_tensor("carry", [128, NE], F32))
        carry2 = E(nc.sbuf_tensor("carry2", [128, NE], F32))
        stats = E(nc.sbuf_tensor("stats", [128, 16], F32))
        xnT = E(nc.sbuf_tensor("xnT", [128, NE * T], BF16))

        AB = E(nc.sbuf_tensor("AB", [128, 59392], BF16))
        AFa = E(nc.sbuf_tensor("AF", [128, 5120], F32))
        P = [E(nc.psum_tensor(f"P{i}", [128, 1024], F32)) for i in range(4)]


        zeros_rdy = B.do("gpsimd", lambda e: e.memset(zeros[:], 0.0), gp, 1)
        ms2 = B.do("gpsimd", lambda e: e.memset(ident[:], 0.0), gp, 1)
        B.wait("gpsimd", gp, ms2)
        ident_rdy = B.do("gpsimd", lambda e: e.affine_select(
            out=ident[:], in_=ident[:], compare_op=ALU.not_equal,
            fill=1.0, base=0, pattern=[[-1, 128]], channel_multiplier=1), gp, 1)

        xln = AFa[:, 0:4096]
        sqscr = AB[:, 0:2048]

        x_ld = [None] * NT
        ln_vA = [None] * NT
        ln_vC = [None] * NT
        ln_scD = [None] * NT
        tr_pe = [[None, None] for _ in range(NT)]
        tr_ve = [[None, None] for _ in range(NT)]

        xnT_r = xnT[:, :].rearrange("p (c n) -> p c n", n=T)

        kv = AB[:, 0:32768]
        qv = AB[:, 32768:49152]
        wsl = AB[:, 49152:57344]
        scol = AB[:, 57344:59392]
        ksb = AB[:, 32768:34816]
        kvt = AFa[:, 2048:3072]
        pf8 = [None] * 8
        wfree = [None, None]
        wqfree = [None, None, None, None]
        ring = [0]
        ring8 = [0]

        def pwait_full(eng, r):
            B.waitp(eng, pf8[2 * r])
            B.waitp(eng, pf8[2 * r + 1])

        def pset_full(r, pair):
            pf8[2 * r] = pair
            pf8[2 * r + 1] = pair
        stt_tgt = {}
        kvseq = [0]
        kvA = [0]
        xnT_sb0_ve = [None]

        def load_wslice(b, dram, sl, pe_free):
            B.waitp("gpsimd", pe_free)
            src = dram[sl, :, :]
            dst = wsl[:, b * 4096:(b + 1) * 4096]
            return GP.issue("gpsimd", lambda e, s=src, d=dst: e.dma_start(out=d, in_=s))

        def load_wslot(slot, dram, sl, pe_free):
            B.waitp("gpsimd", pe_free)
            src = dram[sl, :, :]
            dst = wsl[:, slot * 2048:(slot + 1) * 2048]
            return GP.issue("gpsimd", lambda e, s=src, d=dst: e.dma_start(out=d, in_=s))

        def emit_kv_chunk(ec, sb, r0, r1):
            seq = kvseq[0]; kvseq[0] += 1
            kslot = seq % 2
            vslot = 2 + seq % 2
            wk_rdy = load_wslot(kslot, d_wkT, ec, wqfree[kslot])
            wv_rdy = load_wslot(vslot, d_wvT, ec, wqfree[vslot])
            koff = ec * T + sb * SBT
            B.waitp("tensor", wk_rdy)
            B.waitp("tensor", wv_rdy)
            pwait_full("tensor", r0)
            if sb == 0:
                B.waitp("tensor", xnT_sb0_ve[0])
            for dc in range(ND):
                lhs = wsl[:, kslot * 2048 + dc * 128: kslot * 2048 + dc * 128 + 128]
                for h in range(2):
                    rhs = xnT[:, dc * T + sb * SBT + h * 512:
                              dc * T + sb * SBT + h * 512 + 512]
                    B.do("tensor", lambda e, o=P[r0][:, h * 512:(h + 1) * 512],
                         l=lhs, r=rhs, st=(dc == 0), sp=(dc == ND - 1):
                         e.matmul(o, l, r, start=st, stop=sp))
            k_pe = B.inc_last("tensor", pe, 1)
            pwait_full("tensor", r1)
            for dc in range(ND):
                lhs = wsl[:, vslot * 2048 + dc * 128: vslot * 2048 + dc * 128 + 128]
                for h in range(2):
                    rhs = xnT[:, dc * T + sb * SBT + h * 512:
                              dc * T + sb * SBT + h * 512 + 512]
                    B.do("tensor", lambda e, o=P[r1][:, h * 512:(h + 1) * 512],
                         l=lhs, r=rhs, st=(dc == 0), sp=(dc == ND - 1):
                         e.matmul(o, l, r, start=st, stop=sp))
            v_pe = B.inc_last("tensor", pe, 1)
            wqfree[kslot] = (pe, v_pe)
            wqfree[vslot] = (pe, v_pe)

            kview = ksb[:, (seq % 2) * 1024:(seq % 2) * 1024 + 1024]
            B.wait("scalar", pe, k_pe)
            B.wait("scalar", sc, sc.n)
            if seq >= 2:
                B.wait("scalar", ve, stt_tgt[seq - 2])
            kev = B.do("scalar", lambda e, kv_=kview, r0=r0:
                       e.copy(kv_, P[r0][:, :]), sc, 1)
            pset_full(r0, (sc, kev))

            B.wait("vector", pe, v_pe)
            B.wait("vector", sc, kev)
            B.wait("vector", ve, ve.n)
            tt = tot0 if sb == 0 else tot1
            stt = B.do("vector", lambda e, kv_=kview, r1=r1, koff=koff,
                       tt=tt, ec=ec: e.scalar_tensor_tensor(
                           kv[:, koff:koff + SBT], kv_, 1.0, P[r1][:, :],
                           ALU.mult, ALU.mult,
                           accum_out=tt[:, ec:ec + 1]), ve, 1)
            pset_full(r1, (ve, stt))
            stt_tgt[seq] = stt

        def vstep(fn):
            B.wait("vector", ve, ve.n)
            return B.do("vector", fn, ve, 1)

        def ln_views(t):
            xb = xln[:, (t % 2) * D:(t % 2) * D + D]
            xnbf = AB[:, 2048 + (t % 2) * 2048: 2048 + (t % 2) * 2048 + 2048]
            sb8 = (t % 2) * 8
            st = [stats[:, sb8 + i:sb8 + i + 1] for i in range(8)]
            return xb, xnbf, st

        def ln_stage1(t):
            xb, xnbf, st = ln_views(t)
            s2 = st[1]
            if t >= 2:
                B.wait("sync", sc, ln_scD[t - 2])
            x_ld[t] = SP.issue("sync", lambda e, t=t, xb=xb: e.dma_start(
                out=xb, in_=d_x[t * 128:(t + 1) * 128, :]))
            B.waitp("scalar", x_ld[t])
            B.wait("scalar", sc, sc.n)
            if t >= 2:
                B.wait("scalar", ve, ln_vA[t - 2])
            ln_sq2[t] = B.do("scalar", lambda e, xb=xb, s2=s2: e.activation(
                sqscr, xb, AF.Square, accum_out=s2), sc, 1)

        def ln_stageA(t):
            xb, xnbf, st = ln_views(t)
            s1, s2, mcol, t2c, varc = st[0], st[1], st[2], st[3], st[4]
            B.waitp("vector", x_ld[t])
            if t >= 2:
                B.wait("vector", sc, ln_scD[t - 2])
            vstep(lambda e, xb=xb, s1=s1: e.reduce_sum(s1, xb, axis=mybir.AxisListType.X))
            vstep(lambda e, mcol=mcol, s1=s1: e.tensor_scalar_mul(mcol, s1, 1.0 / D))
            B.wait("vector", sc, ln_sq2[t])
            vstep(lambda e, t2c=t2c, s2=s2: e.tensor_scalar_mul(t2c, s2, 1.0 / D))
            vstep(lambda e, varc=varc, mcol=mcol: e.tensor_mul(varc, mcol, mcol))
            vstep(lambda e, varc=varc, t2c=t2c: e.tensor_sub(varc, t2c, varc))
            ln_vA[t] = vstep(lambda e, varc=varc: e.tensor_scalar_add(varc, varc, EPS))
            B.wait("scalar", ve, ln_vA[t])
            B.wait("scalar", sc, sc.n)
            ln_sB[t] = B.do("scalar", lambda e, sdc=st[5], varc=varc:
                            e.sqrt(sdc, varc), sc, 1)

        def ln_stageB(t):
            xb, xnbf, st = ln_views(t)
            mcol, sdc, rstd, nbc = st[2], st[5], st[6], st[7]
            B.wait("vector", sc, ln_sB[t])
            vstep(lambda e, rstd=rstd, sdc=sdc: e.reciprocal(rstd, sdc))
            ln_vC[t] = vstep(lambda e, nbc=nbc, mcol=mcol, rstd=rstd: e.scalar_tensor_tensor(
                nbc, mcol, -1.0, rstd, ALU.mult, ALU.mult))

            B.wait("scalar", ve, ln_vC[t])
            B.wait("scalar", sc, sc.n)
            if t >= 2:
                B.wait("scalar", pe, tr_pe[t - 2][1])
            ln_scD[t] = B.do("scalar", lambda e, xb=xb, xnbf=xnbf, nbc=nbc,
                             rstd=rstd: e.activation(
                xnbf, xb, AF.Identity, bias=nbc, scale=rstd), sc, 1)

            B.wait("tensor", sc, ln_scD[t])
            B.wait("tensor", gp, ident_rdy)
            for g in range(2):
                if t >= 1:
                    B.waitp("tensor", tr_ve[t - 1][g])
                for j in range(8):
                    c = g * 8 + j
                    B.do("tensor", lambda e, g=g, j=j, c=c, xnbf=xnbf: e.matmul(
                        P[g][:, j * 128:(j + 1) * 128],
                        xnbf[:, c * 128:(c + 1) * 128], ident[:],
                        start=True, stop=True))
                tr_pe[t][g] = B.inc_last("tensor", pe, 1)

                out_ap = xnT_r[:, g * 8:(g + 1) * 8, t * 128:(t + 1) * 128]
                in_ap = P[g][:, 0:1024].rearrange("p (j n) -> p j n", n=128)
                B.wait("vector", pe, tr_pe[t][g])
                B.wait("vector", ve, ve.n)
                tr_ve[t][g] = (ve, B.do("vector", lambda e, o=out_ap, i=in_ap:
                                        e.tensor_copy(o, i), ve, 1))

            if t == 7:
                xnT_sb0_ve[0] = tr_ve[7][1]
            if t >= 8 and _PHASE >= 2:
                emit_kv_chunk(8 + kvA[0], 0, 2, 3)
                kvA[0] += 1

        ln_sq2 = [None] * NT
        ln_sB = [None] * NT
        ln_stage1(0)
        ln_stage1(1)
        ln_stageA(0)
        for t in range(NT):
            ln_stageB(t)
            if t + 2 < NT:
                ln_stage1(t + 2)
            if t + 1 < NT:
                ln_stageA(t + 1)
        sg_ld = SP.issue("sync", lambda e: e.dma_start(out=sgb[:], in_=d_sg))
        su_ld = SP.issue("sync", lambda e: e.dma_start(out=sub[:], in_=d_su))
        mask_ld = SP.issue("sync", lambda e: e.dma_start(out=maskb[:], in_=d_mask))

        if _PHASE < 2:
            for s_ in SP.sems + GP.sems:
                B.wait("sync", s_, s_.n)
            with nc.Block() as block:
                B.install(block)
            return nc

        for ec in range(8):
            emit_kv_chunk(ec, 0, 2, 3)

        pset_full(0, tr_ve[NT - 1][0])
        pset_full(1, tr_ve[NT - 1][1])
        for ec in range(NE):
            r0 = ring[0] % 4; ring[0] += 1
            r1 = ring[0] % 4; ring[0] += 1
            emit_kv_chunk(ec, 1, r0, r1)
        wfree[0] = (pe, pe.n)
        wfree[1] = (pe, pe.n)

        if _PHASE < 3:
            for s_ in SP.sems + GP.sems:
                B.wait("sync", s_, s_.n)
            B.barrier(allsems)
            with nc.Block() as block:
                B.install(block)
            return nc

        B.wait("vector", ve, ve.n)
        totA_ve = B.do("vector", lambda e: e.tensor_add(totA[:], tot0[:], tot1[:]), ve, 1)
        B.wait("sync", ve, totA_ve)
        ccw = SP.issue("sync", lambda e: e.dma_start(out=t_ccin.ap(), in_=totA[:]))
        B.waitp("gpsimd", ccw)
        if not _NO_CC:
            B.do("gpsimd", lambda e: e.collective_compute(
                "AllGather", ALU.bypass, replica_groups=[list(range(8))],
                ins=[t_ccin.ap().opt()], outs=[t_ccout.ap().opt()]), cc, 1)
        carry_ve = None

        if _PHASE < 4:
            for s_ in SP.sems + GP.sems:
                B.wait("sync", s_, s_.n)
            B.barrier(allsems)
            with nc.Block() as block:
                B.install(block)
            return nc

        s_wr = [None] * 16
        wo_done_pe = None
        for sb in range(2):
            wqfree[0] = wqfree[1] = wfree[0]
            wqfree[2] = wqfree[3] = wfree[1]
            qv_sc = [None] * NE
            for sl in range(NE):
                slot = sl % 4
                w_rdy = load_wslot(slot, d_wqT, sl, wqfree[slot])
                ec = sl
                r0 = ring[0] % 4; ring[0] += 1
                B.waitp("tensor", w_rdy)
                pwait_full("tensor", r0)
                for dc in range(ND):
                    lhs = wsl[:, slot * 2048 + dc * 128: slot * 2048 + dc * 128 + 128]
                    for h in range(2):
                        rhs = xnT[:, dc * T + sb * SBT + h * 512:
                                  dc * T + sb * SBT + h * 512 + 512]
                        B.do("tensor", lambda e, o=P[r0][:, h * 512:(h + 1) * 512],
                             l=lhs, r=rhs, st=(dc == 0), sp=(dc == ND - 1):
                             e.matmul(o, l, r, start=st, stop=sp))
                q_pe = B.inc_last("tensor", pe, 1)
                wqfree[slot] = (pe, q_pe)
                B.wait("scalar", pe, q_pe)
                B.wait("scalar", sc, sc.n)
                if ec < 2:
                    B.wait("scalar", ve, max(stt_tgt.values()))
                if sb == 1:
                    B.wait("scalar", pe, wo_done_pe)
                qv_sc[ec] = B.do("scalar", lambda e, ec=ec, r0=r0: e.activation(
                    qv[:, ec * SBT:(ec + 1) * SBT], P[r0][:, :],
                    AF.Identity, scale=DENOM), sc, 1)
                pset_full(r0, (sc, qv_sc[ec]))
            wfree[0] = (pe, pe.n)
            wfree[1] = (pe, pe.n)

            if DEBUG and sb == 0:
                B.wait("sync", sc, qv_sc[NE - 1])
                SP.issue("sync", lambda e: e.dma_start(out=d_dbg_q0, in_=qv))

            if sb == 0:
                B.wait("sync", cc, 1)
                g_lds = []
                for r in range(8):
                    g_lds.append(SP.issue("sync", lambda e, r=r: e.dma_start(
                        out=gath[:, r * NE:(r + 1) * NE],
                        in_=t_ccout.ap()[r * 128:(r + 1) * 128, :])))
                for t_ in g_lds:
                    B.waitp("vector", t_)
                B.waitp("vector", mask_ld)
                B.wait("vector", ve, ve.n)
                B.do("vector", lambda e: e.tensor_scalar(
                    carry[:], gath[:, 0:NE], maskb[:, 0:1], None, ALU.mult), ve, 1)
                for r in range(1, 8):
                    B.wait("vector", ve, ve.n)
                    carry_ve = B.do("vector", lambda e, r=r: e.scalar_tensor_tensor(
                        carry[:], gath[:, r * NE:(r + 1) * NE], maskb[:, r:r + 1],
                        carry[:], ALU.mult, ALU.add), ve, 1)

            vhat_ve = None
            B.wait("vector", gp, zeros_rdy)
            B.wait("vector", ve, max(stt_tgt.values()))
            if carry_ve:
                B.wait("vector", ve, carry_ve)
            for ec in range(NE):
                koff = ec * T + sb * SBT
                init = carry[:, ec:ec + 1] if sb == 0 else carry2[:, ec:ec + 1]
                B.wait("vector", sc, qv_sc[ec])
                B.wait("vector", ve, ve.n)
                B.do("vector", lambda e, koff=koff, init=init: e.tensor_tensor_scan(
                    kvt, kv[:, koff:koff + SBT], zeros[:, :], init,
                    ALU.add, ALU.add), ve, 1)
                if sb == 0:
                    B.wait("vector", ve, ve.n)
                    B.do("vector", lambda e, ec=ec: e.tensor_copy(
                        carry2[:, ec:ec + 1], kvt[:, SBT - 1:SBT]), ve, 1)
                B.wait("vector", ve, ve.n)
                vhat_ve = B.do("vector", lambda e, ec=ec: e.tensor_mul(
                    qv[:, ec * SBT:(ec + 1) * SBT],
                    qv[:, ec * SBT:(ec + 1) * SBT], kvt), ve, 1)

            B.wait("tensor", ve, vhat_ve)
            for sl in range(8):
                w_rdy = load_wslice(sl % 2, d_woT, sl, wfree[sl % 2])
                sc_evs = None
                gs = sb * 8 + sl
                for tile in range(8):
                    r8 = ring8[0] % 8; ring8[0] += 1
                    pview = P[r8 // 2][:, (r8 % 2) * 512:(r8 % 2) * 512 + 256]
                    B.waitp("tensor", w_rdy)
                    B.waitp("tensor", pf8[r8])
                    for ec in range(NE):
                        lhs = qv[:, ec * SBT + tile * 128: ec * SBT + tile * 128 + 128]
                        rhs = wsl[:, (sl % 2) * 4096 + ec * 256: (sl % 2) * 4096 + ec * 256 + 256]
                        B.do("tensor", lambda e, o=pview, l=lhs, r=rhs,
                             st=(ec == 0), sp=(ec == NE - 1):
                             e.matmul(o, l, r, start=st, stop=sp))
                    wo_pe = B.inc_last("tensor", pe, 1)
                    B.wait("scalar", pe, wo_pe)
                    B.wait("scalar", sc, sc.n)
                    if gs >= 1 and tile == 0:
                        B.waitp("scalar", s_wr[gs - 1])
                    sc_evs = B.do("scalar", lambda e, tile=tile, pview=pview:
                                  e.copy(scol[:, tile * 256: tile * 256 + 256],
                                         pview), sc, 1)
                    pf8[r8] = (sc, sc_evs)
                wfree[sl % 2] = (pe, pe.n)
                B.wait("sync", sc, sc_evs)
                dst = d_sscr[sb * SBT:(sb + 1) * SBT, sl * 256:(sl + 1) * 256] \
                    .rearrange("(tl p) f -> p tl f", p=128)
                src = scol.rearrange("p (tl f) -> p tl f", f=256)
                s_wr[gs] = SP.issue("sync", lambda e, d=dst, s=src:
                                    e.dma_start(out=d, in_=s))
            wo_done_pe = pe.n

        if DEBUG:
            B.wait("sync", ve, ve.n)
            B.wait("sync", sc, sc.n)
            SP.issue("sync", lambda e: e.dma_start(out=d_dbg_xnT, in_=xnT[:, :]))
            SP.issue("sync", lambda e: e.dma_start(out=d_dbg_kv, in_=kv))
            SP.issue("sync", lambda e: e.dma_start(out=d_dbg_carry, in_=carry[:, :]))
            SP.issue("sync", lambda e: e.dma_start(out=d_dbg_tot, in_=totA[:, :]))

        B.wait("gpsimd", pe, wo_done_pe)
        B.wait("sync", pe, wo_done_pe)
        B.wait("vector", pe, wo_done_pe)
        B.wait("scalar", pe, wo_done_pe)

        if _PHASE < 5:
            for s_ in SP.sems + GP.sems:
                B.wait("sync", s_, s_.n)
            with nc.Block() as block:
                B.install(block)
            return nc

        hbuf = AB[:, 0:32768]
        wfsl = AB[:, 32768:49152]
        shalf = AB[:, 49152:51200]
        gfin = AB[:, 51200:55296]
        mbuf = AB[:, 55296:57344]
        sdbc = AB[:, 57344:59392]
        xhalf = AFa[:, 0:2048]
        ohalf = AFa[:, 2048:4096]
        t0b = AFa[:, 4096:5120]

        ffree = [(pe, wo_done_pe)] * 4
        fslot = [0]

        B.waitp("gpsimd", s_wr[15])
        sd_ld = GP.issue("gpsimd", lambda e: e.dma_start(out=sdbc, in_=d_sd))

        chain_idx = [0]
        ch_ev2 = {}; ch_ev3 = {}; ch_ev4 = {}; ch_ev5 = {}
        ch_xld = {}; ch_sld = {}
        tails = []

        def flush_tails():
            for fn in tails:
                fn()
            tails.clear()

        def emit_tail(n, gt, dh):
            if n >= 2:
                B.wait("sync", ve, ch_ev4[n - 2])
                B.wait("sync", ve, ch_ev3[n - 2])
            ch_xld[n] = SP.issue("sync", lambda e, gt=gt, dh=dh, n=n: e.dma_start(
                out=xhalf[:, (n % 2) * 1024:(n % 2) * 1024 + 1024],
                in_=d_x[gt * 128:(gt + 1) * 128, dh * 1024:(dh + 1) * 1024]))
            ch_sld[n] = SP.issue("sync", lambda e, gt=gt, dh=dh, n=n: e.dma_start(
                out=shalf[:, (n % 2) * 1024:(n % 2) * 1024 + 1024],
                in_=d_sscr[gt * 128:(gt + 1) * 128, dh * 1024:(dh + 1) * 1024]))
            B.wait("vector", sc, ch_ev2[n])
            B.waitp("vector", ch_sld[n])
            B.wait("vector", ve, ve.n)
            ch_ev3[n] = B.do("vector", lambda e, n=n: e.tensor_mul(
                mbuf[:, (n % 2) * 1024:(n % 2) * 1024 + 1024],
                gfin[:, (n % 4) * 1024:(n % 4) * 1024 + 1024],
                shalf[:, (n % 2) * 1024:(n % 2) * 1024 + 1024]), ve, 1)
            B.waitp("vector", ch_xld[n])
            if n >= 2:
                B.waitp("vector", ch_ev5[n - 2])
            B.wait("vector", ve, ve.n)
            ch_ev4[n] = B.do("vector", lambda e, n=n: e.tensor_add(
                ohalf[:, (n % 2) * 1024:(n % 2) * 1024 + 1024],
                mbuf[:, (n % 2) * 1024:(n % 2) * 1024 + 1024],
                xhalf[:, (n % 2) * 1024:(n % 2) * 1024 + 1024]), ve, 1)
            B.wait("sync", ve, ch_ev4[n])
            ch_ev5[n] = SP.issue("sync", lambda e, gt=gt, dh=dh, n=n: e.dma_start(
                out=d_out[gt * 128:(gt + 1) * 128, dh * 1024:(dh + 1) * 1024],
                in_=ohalf[:, (n % 2) * 1024:(n % 2) * 1024 + 1024]))

        for tb in range(4):
            up_last_ve = None
            hev_gate = [None] * NIC
            for mat in range(2):
                dram = d_wgT if mat == 0 else d_wuT
                scl = sgb if mat == 0 else sub

                def gate_ldwfx(sl, dram=dram):
                    b = fslot[0] % 4; fslot[0] += 1
                    B.waitp("gpsimd", ffree[b])
                    src = dram[sl, :, :]
                    dst = wfsl[:, b * 4096:(b + 1) * 4096]
                    wld = GP.issue("gpsimd", lambda e, s=src, d=dst:
                                   e.dma_start(out=d, in_=s))
                    B.waitp("vector", wld)
                    B.waitp("vector", ffree[b])
                    B.wait("vector", ve, ve.n)
                    wfx = B.do("vector", lambda e, b=b: e.tensor_scalar_add(
                        wfsl[:, b * 4096:(b + 1) * 4096],
                        wfsl[:, b * 4096:(b + 1) * 4096], -128.0), ve, 1)
                    return b, wfx

                cur = gate_ldwfx(0)
                for sl in range(32):
                    b, wfx = cur
                    cur = gate_ldwfx(sl + 1) if sl + 1 < 32 else None
                    if mat == 0 and sl == 2:
                        flush_tails()
                    last_mm = None
                    for j in range(2):
                        ic = sl * 2 + j
                        r8 = ring8[0] % 8; ring8[0] += 1
                        pview = P[r8 // 2][:, (r8 % 2) * 512:(r8 % 2) * 512 + TB]
                        B.wait("tensor", ve, wfx)
                        B.waitp("tensor", pf8[r8])
                        for dc in range(ND):
                            lhs = wfsl[:, b * 4096 + dc * 256 + j * 128:
                                       b * 4096 + dc * 256 + j * 128 + 128]
                            rhs = xnT[:, dc * T + tb * TB: dc * T + tb * TB + TB]
                            B.do("tensor", lambda e, o=pview, l=lhs, r=rhs,
                                 st=(dc == 0), sp=(dc == ND - 1):
                                 e.matmul(o, l, r, start=st, stop=sp))
                        g_pe = B.inc_last("tensor", pe, 1)
                        last_mm = g_pe
                        if mat == 0:
                            B.wait("scalar", pe, g_pe)
                            B.wait("scalar", sc, sc.n)
                            hev = B.do("scalar", lambda e, ic=ic, pview=pview, scl=scl:
                                       e.activation(hbuf[:, ic * TB:(ic + 1) * TB],
                                                    pview, AF.Silu,
                                                    scale=scl[:, ic:ic + 1]), sc, 1)
                            pf8[r8] = (sc, hev)
                            hev_gate[ic] = hev
                        else:
                            B.wait("vector", pe, g_pe)
                            B.wait("vector", sc, hev_gate[ic])
                            B.wait("vector", ve, ve.n)
                            hev = B.do("vector", lambda e, ic=ic, pview=pview, scl=scl:
                                       e.scalar_tensor_tensor(
                                           hbuf[:, ic * TB:(ic + 1) * TB],
                                           pview, scl[:, ic:ic + 1],
                                           hbuf[:, ic * TB:(ic + 1) * TB],
                                           ALU.mult, ALU.mult), ve, 1)
                            pf8[r8] = (ve, hev)
                            up_last_ve = hev
                    ffree[b] = (pe, last_mm)

            B.wait("tensor", ve, up_last_ve)
            for dh in range(2):
                for r8_ in range(8):
                    B.waitp("tensor", pf8[r8_])

                def down_ldwfx(sl, dh=dh):
                    b = fslot[0] % 4; fslot[0] += 1
                    B.waitp("gpsimd", ffree[b])
                    src = d_wdT[dh, sl, :, :]
                    dst = wfsl[:, b * 4096:(b + 1) * 4096]
                    wld = GP.issue("gpsimd", lambda e, s=src, d=dst:
                                   e.dma_start(out=d, in_=s))
                    B.waitp("vector", wld)
                    B.waitp("vector", ffree[b])
                    B.wait("vector", ve, ve.n)
                    wfx = B.do("vector", lambda e, b=b: e.tensor_scalar_add(
                        wfsl[:, b * 4096:(b + 1) * 4096],
                        wfsl[:, b * 4096:(b + 1) * 4096], -128.0), ve, 1)
                    return b, wfx

                stop_pe = None
                curd = down_ldwfx(0)
                for sl in range(16):
                    b, wfx = curd
                    curd = down_ldwfx(sl + 1) if sl + 1 < 16 else None
                    B.wait("tensor", ve, wfx)
                    if sl == 2:
                        flush_tails()
                    for c4 in range(4):
                        for tile in range(4):
                            lhs = hbuf[:, (sl * 4 + c4) * TB + tile * 128:
                                       (sl * 4 + c4) * TB + tile * 128 + 128]
                            for h in range(2):
                                rhs = wfsl[:, b * 4096 + c4 * 1024 + h * 512:
                                           b * 4096 + c4 * 1024 + h * 512 + 512]
                                st = (sl == 0 and c4 == 0)
                                sp = (sl == 15 and c4 == 3)
                                B.do("tensor", lambda e,
                                     o=P[tile][:, h * 512:(h + 1) * 512],
                                     l=lhs, r=rhs, st=st, sp=sp:
                                     e.matmul(o, l, r, start=st, stop=sp))
                    slot_pe = B.inc_last("tensor", pe, 1)
                    ffree[b] = (pe, slot_pe)
                    if sl == 15:
                        stop_pe = slot_pe

                for tile in range(4):
                    n = chain_idx[0]; chain_idx[0] += 1
                    gt = tb * 4 + tile
                    B.wait("vector", pe, stop_pe)
                    B.waitp("vector", sd_ld)
                    B.wait("vector", ve, ve.n)
                    ev1 = B.do("vector", lambda e, tile=tile, dh=dh: e.tensor_mul(
                        P[tile][:, 0:1024], P[tile][:, 0:1024],
                        sdbc[:, dh * 1024:(dh + 1) * 1024]), ve, 1)
                    B.wait("scalar", ve, ev1)
                    B.wait("scalar", sc, sc.n)
                    if n >= 4:
                        B.wait("scalar", ve, ch_ev3[n - 4])
                    ch_ev2[n] = B.do("scalar", lambda e, tile=tile, n=n: e.activation(
                        gfin[:, (n % 4) * 1024:(n % 4) * 1024 + 1024],
                        P[tile][:, 0:1024], AF.Silu), sc, 1)
                    pf8[2 * tile] = (sc, ch_ev2[n])
                    pf8[2 * tile + 1] = (sc, ch_ev2[n])
                    if tb == 3 and dh == 1:
                        emit_tail(n, gt, dh)
                    else:
                        tails.append(lambda n=n, gt=gt, dh=dh: emit_tail(n, gt, dh))

        flush_tails()

        for s in SP.sems + GP.sems:
            B.wait("sync", s, s.n)

        with nc.Block() as block:
            B.install(block)

    return nc


def _host_prep(L0_attn_x, L0_attn_Wq, L0_attn_Wk, L0_attn_Wv, L0_attn_Wo,
               L0_ffn_gate_w, L0_ffn_gate_scale, L0_ffn_up_w, L0_ffn_up_scale,
               L0_ffn_down_w, L0_ffn_down_scale):
    x = np.asarray(L0_attn_x, dtype=np.float32)
    def tile_attn128(w):
        wt = np.asarray(w, dtype=np.float32).T.reshape(16, 128, 16, 128)
        return np.ascontiguousarray(wt.transpose(2, 1, 0, 3).reshape(16, 128, 2048))

    def tile_attn256(w):
        wt = np.asarray(w, dtype=np.float32).T.reshape(16, 128, 8, 256)
        return np.ascontiguousarray(wt.transpose(2, 1, 0, 3).reshape(8, 128, 4096))

    wqT = tile_attn128(L0_attn_Wq)
    wkT = tile_attn128(L0_attn_Wk)
    wvT = tile_attn128(L0_attn_Wv)
    woT = tile_attn256(L0_attn_Wo)

    def tile_gate(w):
        wt = np.asarray(w).astype(np.uint8).T.reshape(16, 128, 32, 256)
        return np.ascontiguousarray(wt.transpose(2, 1, 0, 3).reshape(32, 128, 4096))

    wgT = tile_gate(L0_ffn_gate_w)
    wuT = tile_gate(L0_ffn_up_w)
    wdt = np.asarray(L0_ffn_down_w).astype(np.uint8).T.reshape(16, 4, 128, 2, 1024)
    wdT = np.ascontiguousarray(wdt.transpose(3, 0, 2, 1, 4).reshape(2, 16, 128, 4096))
    sg = np.ascontiguousarray(np.asarray(L0_ffn_gate_scale, dtype=np.float32)
                              .reshape(NIC, 128).T)
    su = np.ascontiguousarray(np.asarray(L0_ffn_up_scale, dtype=np.float32)
                              .reshape(NIC, 128).T)
    sd = np.ascontiguousarray(np.broadcast_to(
        np.asarray(L0_ffn_down_scale, dtype=np.float32), (128, D)))

    in_maps = []
    for c in range(8):
        b, j = c // 4, c % 4
        xs = np.ascontiguousarray(x[b, j * T:(j + 1) * T, :])
        mvec = np.zeros(8, dtype=np.float32)
        for r in range(8):
            if r // 4 == b and r % 4 < j:
                mvec[r] = 1.0
        mask = np.ascontiguousarray(np.broadcast_to(mvec, (128, 8)))
        in_maps.append(dict(x=xs, wqT=wqT, wkT=wkT, wvT=wvT, woT=woT,
                            wgT=wgT, wuT=wuT, wdT=wdT, sg=sg, su=su, sd=sd,
                            mask=mask))
    return in_maps


def kernel(**inputs):
    in_maps = _host_prep(**inputs)
    nc = _build_nc()
    res = run_bass_kernel_spmd(nc, in_maps, core_ids=list(range(8)))
    if DEBUG:
        kernel.last_results = res
    full = np.empty((2, 8192, D), dtype=np.float32)
    for c in range(8):
        b, j = c // 4, c % 4
        full[b, j * T:(j + 1) * T, :] = np.asarray(res.results[c]["out"],
                                                   dtype=np.float32)
    return full


# version 16 (final)
# speedup vs baseline: 1.0055x; 1.0019x over previous
import numpy as np
from contextlib import ExitStack
import concourse.bass as bass
import concourse.mybir as mybir
from concourse.bass_utils import run_bass_kernel_spmd

F32 = mybir.dt.float32
BF16 = mybir.dt.bfloat16
U8 = mybir.dt.uint8
AF = mybir.ActivationFunctionType
ALU = mybir.AluOpType

D = 2048
II = 8192
T = 2048
NT = 16
NE = 16
ND = 16
NIC = 64
SBT = 1024
TB = 512
DENOM = float(np.sqrt(np.float32(2048.0)) / np.float32(8192.0))
EPS = 1e-5

DEBUG = False
_NO_CC = False
_PHASE = 5


class Sem:
    __slots__ = ("h", "n")

    def __init__(self, h):
        self.h = h
        self.n = 0


class Build:
    ENGS = ("sync", "gpsimd", "scalar", "vector", "tensor")

    def __init__(self):
        self.ops = {e: [] for e in self.ENGS}
        self._waited = {e: {} for e in self.ENGS}

    def wait(self, eng, sem, val):
        if val is None or val <= 0:
            return
        w = self._waited[eng]
        if w.get(id(sem), -1) >= val:
            return
        w[id(sem)] = val
        self.ops[eng].append(lambda e, h=sem.h, v=val: e.wait_ge(h, v))

    def waitp(self, eng, pair):
        if pair is not None:
            self.wait(eng, pair[0], pair[1])

    def do(self, eng, fn, sem=None, k=1):
        if sem is None:
            self.ops[eng].append(fn)
            return None
        sem.n += k
        tgt = sem.n
        self.ops[eng].append(lambda e, f=fn, h=sem.h, kk=k: f(e).then_inc(h, kk))
        return tgt

    def inc_last(self, eng, sem, k=1):
        sem.n += k
        tgt = sem.n
        prev = self.ops[eng].pop()
        self.ops[eng].append(lambda e, p=prev, h=sem.h, kk=k: p(e).then_inc(h, kk))
        return tgt

    def barrier(self, sems):
        snap = [(s, s.n) for s in sems]
        for eng in self.ENGS:
            for s, v in snap:
                self.wait(eng, s, v)

    def install(self, block):
        for name in self.ENGS:
            ops = self.ops[name]

            def body(e, ops=ops):
                for fn in ops:
                    fn(e)

            getattr(block, name)(body)


class DmaPool:

    def __init__(self, B, sems):
        self.B = B
        self.sems = sems
        self.i = 0

    def issue(self, eng, fn):
        s = self.sems[self.i % len(self.sems)]
        self.i += 1
        self.B.wait(eng, s, s.n)
        tgt = self.B.do(eng, fn, s, 16)
        return (s, tgt)


def _build_nc():
    nc = bass.Bass("TRN2", target_bir_lowering=False, detect_race_conditions=False)

    d_x = nc.dram_tensor("x", [T, D], F32, kind="ExternalInput").ap()
    d_wqT = nc.dram_tensor("wqT", [16, 128, 2048], F32, kind="ExternalInput").ap()
    d_wkT = nc.dram_tensor("wkT", [16, 128, 2048], F32, kind="ExternalInput").ap()
    d_wvT = nc.dram_tensor("wvT", [16, 128, 2048], F32, kind="ExternalInput").ap()
    d_woT = nc.dram_tensor("woT", [8, 128, 4096], F32, kind="ExternalInput").ap()
    d_wgT = nc.dram_tensor("wgT", [32, 128, 4096], U8, kind="ExternalInput").ap()
    d_wuT = nc.dram_tensor("wuT", [32, 128, 4096], U8, kind="ExternalInput").ap()
    d_wdT = nc.dram_tensor("wdT", [2, 16, 128, 4096], U8, kind="ExternalInput").ap()
    d_sg = nc.dram_tensor("sg", [128, NIC], F32, kind="ExternalInput").ap()
    d_su = nc.dram_tensor("su", [128, NIC], F32, kind="ExternalInput").ap()
    d_sd = nc.dram_tensor("sd", [128, D], F32, kind="ExternalInput").ap()
    d_mask = nc.dram_tensor("mask", [128, 8], F32, kind="ExternalInput").ap()
    d_out = nc.dram_tensor("out", [T, D], F32, kind="ExternalOutput").ap()

    s_kind = "ExternalOutput" if DEBUG else "Internal"
    d_sscr = nc.dram_tensor("s_scr", [T, D], BF16, kind=s_kind).ap()
    t_ccin = nc.dram_tensor("cc_in", [128, NE], F32)
    t_ccout = nc.dram_tensor("cc_out", [128 * 8, NE], F32)
    if DEBUG:
        d_dbg_xnT = nc.dram_tensor("dbg_xnT", [128, NE * T], BF16, kind="ExternalOutput").ap()
        d_dbg_kv = nc.dram_tensor("dbg_kv", [128, NE * T], BF16, kind="ExternalOutput").ap()
        d_dbg_carry = nc.dram_tensor("dbg_carry", [128, NE], F32, kind="ExternalOutput").ap()
        d_dbg_tot = nc.dram_tensor("dbg_tot", [128, NE], F32, kind="ExternalOutput").ap()
        d_dbg_q0 = nc.dram_tensor("dbg_q0", [128, NE * SBT], BF16, kind="ExternalOutput").ap()

    B = Build()
    es = ExitStack()
    with es:
        E = es.enter_context
        pe = Sem(E(nc.semaphore("pe")))
        sc = Sem(E(nc.semaphore("sc")))
        ve = Sem(E(nc.semaphore("ve")))
        gp = Sem(E(nc.semaphore("gp")))
        cc = Sem(E(nc.semaphore("cc")))
        SP = DmaPool(B, [Sem(E(nc.semaphore(f"dp{i}"))) for i in range(12)])
        GP = DmaPool(B, [Sem(E(nc.semaphore(f"dq{i}"))) for i in range(12)])
        allsems = [pe, sc, ve, gp] + SP.sems + GP.sems

        ident = E(nc.sbuf_tensor("ident", [128, 128], BF16))
        zeros = E(nc.sbuf_tensor("zeros", [128, SBT], BF16))
        sgb = E(nc.sbuf_tensor("sgb", [128, NIC], F32))
        sub = E(nc.sbuf_tensor("sub", [128, NIC], F32))
        maskb = E(nc.sbuf_tensor("maskb", [128, 8], F32))
        tot0 = E(nc.sbuf_tensor("tot0", [128, NE], F32))
        tot1 = E(nc.sbuf_tensor("tot1", [128, NE], F32))
        totA = E(nc.sbuf_tensor("totA", [128, NE], F32))
        gath = E(nc.sbuf_tensor("gath", [128, 8 * NE], F32))
        carry = E(nc.sbuf_tensor("carry", [128, NE], F32))
        carry2 = E(nc.sbuf_tensor("carry2", [128, NE], F32))
        stats = E(nc.sbuf_tensor("stats", [128, 16], F32))
        xnT = E(nc.sbuf_tensor("xnT", [128, NE * T], BF16))

        AB = E(nc.sbuf_tensor("AB", [128, 59392], BF16))
        AFa = E(nc.sbuf_tensor("AF", [128, 5120], F32))
        P = [E(nc.psum_tensor(f"P{i}", [128, 1024], F32)) for i in range(4)]


        zeros_rdy = B.do("gpsimd", lambda e: e.memset(zeros[:], 0.0), gp, 1)
        ms2 = B.do("gpsimd", lambda e: e.memset(ident[:], 0.0), gp, 1)
        B.wait("gpsimd", gp, ms2)
        ident_rdy = B.do("gpsimd", lambda e: e.affine_select(
            out=ident[:], in_=ident[:], compare_op=ALU.not_equal,
            fill=1.0, base=0, pattern=[[-1, 128]], channel_multiplier=1), gp, 1)

        xln = AFa[:, 0:4096]
        sqscr = AB[:, 0:2048]

        x_ld = [None] * NT
        ln_vA = [None] * NT
        ln_vC = [None] * NT
        ln_scD = [None] * NT
        tr_pe = [[None, None] for _ in range(NT)]
        tr_ve = [[None, None] for _ in range(NT)]

        xnT_r = xnT[:, :].rearrange("p (c n) -> p c n", n=T)

        kv = AB[:, 0:32768]
        qv = AB[:, 32768:49152]
        wsl = AB[:, 49152:57344]
        scol = AB[:, 57344:59392]
        ksb = AB[:, 32768:34816]
        kvt = AFa[:, 2048:3072]
        pf8 = [None] * 8
        wfree = [None, None]
        wqfree = [None, None, None, None]
        ring = [0]
        ring8 = [0]

        def pwait_full(eng, r):
            B.waitp(eng, pf8[2 * r])
            B.waitp(eng, pf8[2 * r + 1])

        def pset_full(r, pair):
            pf8[2 * r] = pair
            pf8[2 * r + 1] = pair
        stt_tgt = {}
        kvseq = [0]
        kvld = [0]
        kvA = [0]
        pend_kv = []
        xnT_sb0_ve = [None]

        def load_wslice(b, dram, sl, pe_free):
            B.waitp("gpsimd", pe_free)
            src = dram[sl, :, :]
            dst = wsl[:, b * 4096:(b + 1) * 4096]
            return GP.issue("gpsimd", lambda e, s=src, d=dst: e.dma_start(out=d, in_=s))

        def load_wslot(slot, dram, sl, pe_free):
            B.waitp("gpsimd", pe_free)
            src = dram[sl, :, :]
            dst = wsl[:, slot * 2048:(slot + 1) * 2048]
            return GP.issue("gpsimd", lambda e, s=src, d=dst: e.dma_start(out=d, in_=s))

        def kv_load(ec):
            seq = kvld[0]; kvld[0] += 1
            kslot = seq % 2
            vslot = 2 + seq % 2
            wk_rdy = load_wslot(kslot, d_wkT, ec, wqfree[kslot])
            wv_rdy = load_wslot(vslot, d_wvT, ec, wqfree[vslot])
            return [kslot, vslot, wk_rdy, wv_rdy]

        def emit_kv_mms(ec, sb, r0, r1, ld):
            seq = kvseq[0]; kvseq[0] += 1
            kslot, vslot, wk_rdy, wv_rdy = ld
            B.waitp("tensor", wk_rdy)
            B.waitp("tensor", wv_rdy)
            pwait_full("tensor", r0)
            if sb == 0:
                B.wait("tensor", ve, xnT_sb0_ve[0])
            for dc in range(ND):
                lhs = wsl[:, kslot * 2048 + dc * 128: kslot * 2048 + dc * 128 + 128]
                for h in range(2):
                    rhs = xnT[:, dc * T + sb * SBT + h * 512:
                              dc * T + sb * SBT + h * 512 + 512]
                    B.do("tensor", lambda e, o=P[r0][:, h * 512:(h + 1) * 512],
                         l=lhs, r=rhs, st=(dc == 0), sp=(dc == ND - 1):
                         e.matmul(o, l, r, start=st, stop=sp))
            k_pe = B.inc_last("tensor", pe, 1)
            pwait_full("tensor", r1)
            for dc in range(ND):
                lhs = wsl[:, vslot * 2048 + dc * 128: vslot * 2048 + dc * 128 + 128]
                for h in range(2):
                    rhs = xnT[:, dc * T + sb * SBT + h * 512:
                              dc * T + sb * SBT + h * 512 + 512]
                    B.do("tensor", lambda e, o=P[r1][:, h * 512:(h + 1) * 512],
                         l=lhs, r=rhs, st=(dc == 0), sp=(dc == ND - 1):
                         e.matmul(o, l, r, start=st, stop=sp))
            v_pe = B.inc_last("tensor", pe, 1)
            return dict(seq=seq, ec=ec, sb=sb, r0=r0, r1=r1,
                        k_pe=k_pe, v_pe=v_pe)

        def emit_kv_epi(st):
            seq, ec, sb = st["seq"], st["ec"], st["sb"]
            r0, r1 = st["r0"], st["r1"]
            koff = ec * T + sb * SBT
            kview = ksb[:, (seq % 2) * 1024:(seq % 2) * 1024 + 1024]
            B.wait("scalar", pe, st["k_pe"])
            B.wait("scalar", sc, sc.n)
            if seq >= 2:
                B.wait("scalar", ve, stt_tgt[seq - 2])
            kev = B.do("scalar", lambda e, kv_=kview, r0=r0:
                       e.copy(kv_, P[r0][:, :]), sc, 1)
            pset_full(r0, (sc, kev))

            B.wait("vector", pe, st["v_pe"])
            B.wait("vector", sc, kev)
            B.wait("vector", ve, ve.n)
            tt = tot0 if sb == 0 else tot1
            stt = B.do("vector", lambda e, kv_=kview, r1=r1, koff=koff,
                       tt=tt, ec=ec: e.scalar_tensor_tensor(
                           kv[:, koff:koff + SBT], kv_, 1.0, P[r1][:, :],
                           ALU.mult, ALU.mult,
                           accum_out=tt[:, ec:ec + 1]), ve, 1)
            pset_full(r1, (ve, stt))
            stt_tgt[seq] = stt

        def vstep(fn):
            B.wait("vector", ve, ve.n)
            return B.do("vector", fn, ve, 1)

        def ln_views(t):
            xb = xln[:, (t % 2) * D:(t % 2) * D + D]
            xnbf = AB[:, 2048 + (t % 2) * 2048: 2048 + (t % 2) * 2048 + 2048]
            sb8 = (t % 2) * 8
            st = [stats[:, sb8 + i:sb8 + i + 1] for i in range(8)]
            return xb, xnbf, st

        def ln_stage1(t):
            xb, xnbf, st = ln_views(t)
            s2 = st[1]
            if t >= 2:
                B.wait("sync", sc, ln_scD[t - 2])
            x_ld[t] = SP.issue("sync", lambda e, t=t, xb=xb: e.dma_start(
                out=xb, in_=d_x[t * 128:(t + 1) * 128, :]))
            B.waitp("scalar", x_ld[t])
            B.wait("scalar", sc, sc.n)
            if t >= 2:
                B.wait("scalar", ve, ln_vA[t - 2])
            ln_sq2[t] = B.do("scalar", lambda e, xb=xb, s2=s2: e.activation(
                sqscr, xb, AF.Square, accum_out=s2), sc, 1)

        def ln_stageA(t):
            xb, xnbf, st = ln_views(t)
            s1, s2, mcol, t2c, varc = st[0], st[1], st[2], st[3], st[4]
            B.waitp("vector", x_ld[t])
            if t >= 2:
                B.wait("vector", sc, ln_scD[t - 2])
            vstep(lambda e, xb=xb, s1=s1: e.reduce_sum(s1, xb, axis=mybir.AxisListType.X))
            vstep(lambda e, mcol=mcol, s1=s1: e.tensor_scalar_mul(mcol, s1, 1.0 / D))
            B.wait("vector", sc, ln_sq2[t])
            vstep(lambda e, t2c=t2c, s2=s2: e.tensor_scalar_mul(t2c, s2, 1.0 / D))
            vstep(lambda e, varc=varc, mcol=mcol: e.tensor_mul(varc, mcol, mcol))
            vstep(lambda e, varc=varc, t2c=t2c: e.tensor_sub(varc, t2c, varc))
            ln_vA[t] = vstep(lambda e, varc=varc: e.tensor_scalar_add(varc, varc, EPS))
            B.wait("scalar", ve, ln_vA[t])
            B.wait("scalar", sc, sc.n)
            ln_sB[t] = B.do("scalar", lambda e, sdc=st[5], varc=varc:
                            e.sqrt(sdc, varc), sc, 1)

        def ln_stageB(t):
            xb, xnbf, st = ln_views(t)
            mcol, sdc, rstd, nbc = st[2], st[5], st[6], st[7]
            B.wait("vector", sc, ln_sB[t])
            vstep(lambda e, rstd=rstd, sdc=sdc: e.reciprocal(rstd, sdc))
            ln_vC[t] = vstep(lambda e, nbc=nbc, mcol=mcol, rstd=rstd: e.scalar_tensor_tensor(
                nbc, mcol, -1.0, rstd, ALU.mult, ALU.mult))

            B.wait("scalar", ve, ln_vC[t])
            B.wait("scalar", sc, sc.n)
            if t >= 2:
                B.wait("scalar", pe, tr_pe[t - 2][1])
            ln_scD[t] = B.do("scalar", lambda e, xb=xb, xnbf=xnbf, nbc=nbc,
                             rstd=rstd: e.activation(
                xnbf, xb, AF.Identity, bias=nbc, scale=rstd), sc, 1)

            if pend_kv:
                emit_kv_epi(pend_kv.pop(0))

            B.wait("tensor", sc, ln_scD[t])
            B.wait("tensor", gp, ident_rdy)
            for g in range(2):
                if t >= 1:
                    B.wait("tensor", ve, tr_ve[t - 1][g])
                for j in range(8):
                    c = g * 8 + j
                    B.do("tensor", lambda e, g=g, j=j, c=c, xnbf=xnbf: e.matmul(
                        P[g][:, j * 128:(j + 1) * 128],
                        xnbf[:, c * 128:(c + 1) * 128], ident[:],
                        start=True, stop=True))
                tr_pe[t][g] = B.inc_last("tensor", pe, 1)

                B.wait("vector", pe, tr_pe[t][g])
                B.wait("vector", ve, ve.n)
                out_ap = xnT_r[:, g * 8:(g + 1) * 8, t * 128:(t + 1) * 128]
                in_ap = P[g][:, 0:1024].rearrange("p (j n) -> p j n", n=128)
                tr_ve[t][g] = B.do("vector", lambda e, o=out_ap, i=in_ap:
                                   e.tensor_copy(o, i), ve, 1)

            if t == 7:
                xnT_sb0_ve[0] = tr_ve[7][1]
            if t >= 8 and _PHASE >= 2:
                ld = kv_load(8 + kvA[0])
                st_ = emit_kv_mms(8 + kvA[0], 0, 2, 3, ld)
                wqfree[ld[0]] = (pe, st_["v_pe"])
                wqfree[ld[1]] = (pe, st_["v_pe"])
                pend_kv.append(st_)
                kvA[0] += 1

        ln_sq2 = [None] * NT
        ln_sB = [None] * NT
        ln_stage1(0)
        ln_stage1(1)
        ln_stageA(0)
        for t in range(NT):
            ln_stageB(t)
            if t + 2 < NT:
                ln_stage1(t + 2)
            if t + 1 < NT:
                ln_stageA(t + 1)
        sg_ld = SP.issue("sync", lambda e: e.dma_start(out=sgb[:], in_=d_sg))
        su_ld = SP.issue("sync", lambda e: e.dma_start(out=sub[:], in_=d_su))
        mask_ld = SP.issue("sync", lambda e: e.dma_start(out=maskb[:], in_=d_mask))

        if _PHASE < 2:
            for s_ in SP.sems + GP.sems:
                B.wait("sync", s_, s_.n)
            with nc.Block() as block:
                B.install(block)
            return nc

        while pend_kv:
            emit_kv_epi(pend_kv.pop(0))

        pset_full(0, (ve, tr_ve[NT - 1][0]))
        pset_full(1, (ve, tr_ve[NT - 1][1]))
        B.wait("tensor", ve, tr_ve[NT - 1][1])
        for ec in range(8):
            ld = kv_load(ec)
            st0 = emit_kv_mms(ec, 0, 0, 1, ld)
            emit_kv_epi(st0)
            st1 = emit_kv_mms(ec, 1, 2, 3, ld)
            wqfree[ld[0]] = (pe, st1["v_pe"])
            wqfree[ld[1]] = (pe, st1["v_pe"])
            emit_kv_epi(st1)
        for i, ec in enumerate(range(8, NE)):
            ld = kv_load(ec)
            r0, r1 = (0, 1) if i % 2 == 0 else (2, 3)
            st_ = emit_kv_mms(ec, 1, r0, r1, ld)
            wqfree[ld[0]] = (pe, st_["v_pe"])
            wqfree[ld[1]] = (pe, st_["v_pe"])
            emit_kv_epi(st_)
        wfree[0] = (pe, pe.n)
        wfree[1] = (pe, pe.n)

        if _PHASE < 3:
            for s_ in SP.sems + GP.sems:
                B.wait("sync", s_, s_.n)
            B.barrier(allsems)
            with nc.Block() as block:
                B.install(block)
            return nc

        B.wait("vector", ve, ve.n)
        totA_ve = B.do("vector", lambda e: e.tensor_add(totA[:], tot0[:], tot1[:]), ve, 1)
        B.wait("sync", ve, totA_ve)
        ccw = SP.issue("sync", lambda e: e.dma_start(out=t_ccin.ap(), in_=totA[:]))
        B.waitp("gpsimd", ccw)
        if not _NO_CC:
            B.do("gpsimd", lambda e: e.collective_compute(
                "AllGather", ALU.bypass, replica_groups=[list(range(8))],
                ins=[t_ccin.ap().opt()], outs=[t_ccout.ap().opt()]), cc, 1)
        carry_ve = None

        if _PHASE < 4:
            for s_ in SP.sems + GP.sems:
                B.wait("sync", s_, s_.n)
            B.barrier(allsems)
            with nc.Block() as block:
                B.install(block)
            return nc

        s_wr = [None] * 16
        wo_done_pe = None
        for sb in range(2):
            wqfree[0] = wqfree[1] = wfree[0]
            wqfree[2] = wqfree[3] = wfree[1]
            qv_sc = [None] * NE
            for sl in range(NE):
                slot = sl % 4
                w_rdy = load_wslot(slot, d_wqT, sl, wqfree[slot])
                ec = sl
                r0 = ring[0] % 4; ring[0] += 1
                B.waitp("tensor", w_rdy)
                pwait_full("tensor", r0)
                for dc in range(ND):
                    lhs = wsl[:, slot * 2048 + dc * 128: slot * 2048 + dc * 128 + 128]
                    for h in range(2):
                        rhs = xnT[:, dc * T + sb * SBT + h * 512:
                                  dc * T + sb * SBT + h * 512 + 512]
                        B.do("tensor", lambda e, o=P[r0][:, h * 512:(h + 1) * 512],
                             l=lhs, r=rhs, st=(dc == 0), sp=(dc == ND - 1):
                             e.matmul(o, l, r, start=st, stop=sp))
                q_pe = B.inc_last("tensor", pe, 1)
                wqfree[slot] = (pe, q_pe)
                B.wait("scalar", pe, q_pe)
                B.wait("scalar", sc, sc.n)
                if ec < 2:
                    B.wait("scalar", ve, max(stt_tgt.values()))
                if sb == 1:
                    B.wait("scalar", pe, wo_done_pe)
                qv_sc[ec] = B.do("scalar", lambda e, ec=ec, r0=r0: e.activation(
                    qv[:, ec * SBT:(ec + 1) * SBT], P[r0][:, :],
                    AF.Identity, scale=DENOM), sc, 1)
                pset_full(r0, (sc, qv_sc[ec]))
            wfree[0] = (pe, pe.n)
            wfree[1] = (pe, pe.n)

            if DEBUG and sb == 0:
                B.wait("sync", sc, qv_sc[NE - 1])
                SP.issue("sync", lambda e: e.dma_start(out=d_dbg_q0, in_=qv))

            if sb == 0:
                B.wait("sync", cc, 1)
                g_lds = []
                for r in range(8):
                    g_lds.append(SP.issue("sync", lambda e, r=r: e.dma_start(
                        out=gath[:, r * NE:(r + 1) * NE],
                        in_=t_ccout.ap()[r * 128:(r + 1) * 128, :])))
                for t_ in g_lds:
                    B.waitp("vector", t_)
                B.waitp("vector", mask_ld)
                B.wait("vector", ve, ve.n)
                B.do("vector", lambda e: e.tensor_scalar(
                    carry[:], gath[:, 0:NE], maskb[:, 0:1], None, ALU.mult), ve, 1)
                for r in range(1, 8):
                    B.wait("vector", ve, ve.n)
                    carry_ve = B.do("vector", lambda e, r=r: e.scalar_tensor_tensor(
                        carry[:], gath[:, r * NE:(r + 1) * NE], maskb[:, r:r + 1],
                        carry[:], ALU.mult, ALU.add), ve, 1)

            vhat_ve = None
            B.wait("vector", gp, zeros_rdy)
            B.wait("vector", ve, max(stt_tgt.values()))
            if carry_ve:
                B.wait("vector", ve, carry_ve)
            for ec in range(NE):
                koff = ec * T + sb * SBT
                init = carry[:, ec:ec + 1] if sb == 0 else carry2[:, ec:ec + 1]
                B.wait("vector", sc, qv_sc[ec])
                B.wait("vector", ve, ve.n)
                B.do("vector", lambda e, koff=koff, init=init: e.tensor_tensor_scan(
                    kvt, kv[:, koff:koff + SBT], zeros[:, :], init,
                    ALU.add, ALU.add), ve, 1)
                if sb == 0:
                    B.wait("vector", ve, ve.n)
                    B.do("vector", lambda e, ec=ec: e.tensor_copy(
                        carry2[:, ec:ec + 1], kvt[:, SBT - 1:SBT]), ve, 1)
                B.wait("vector", ve, ve.n)
                vhat_ve = B.do("vector", lambda e, ec=ec: e.tensor_mul(
                    qv[:, ec * SBT:(ec + 1) * SBT],
                    qv[:, ec * SBT:(ec + 1) * SBT], kvt), ve, 1)

            B.wait("tensor", ve, vhat_ve)
            for sl in range(8):
                w_rdy = load_wslice(sl % 2, d_woT, sl, wfree[sl % 2])
                sc_evs = None
                gs = sb * 8 + sl
                for tile in range(8):
                    r8 = ring8[0] % 8; ring8[0] += 1
                    pview = P[r8 // 2][:, (r8 % 2) * 512:(r8 % 2) * 512 + 256]
                    B.waitp("tensor", w_rdy)
                    B.waitp("tensor", pf8[r8])
                    for ec in range(NE):
                        lhs = qv[:, ec * SBT + tile * 128: ec * SBT + tile * 128 + 128]
                        rhs = wsl[:, (sl % 2) * 4096 + ec * 256: (sl % 2) * 4096 + ec * 256 + 256]
                        B.do("tensor", lambda e, o=pview, l=lhs, r=rhs,
                             st=(ec == 0), sp=(ec == NE - 1):
                             e.matmul(o, l, r, start=st, stop=sp))
                    wo_pe = B.inc_last("tensor", pe, 1)
                    B.wait("scalar", pe, wo_pe)
                    B.wait("scalar", sc, sc.n)
                    if gs >= 1 and tile == 0:
                        B.waitp("scalar", s_wr[gs - 1])
                    sc_evs = B.do("scalar", lambda e, tile=tile, pview=pview:
                                  e.copy(scol[:, tile * 256: tile * 256 + 256],
                                         pview), sc, 1)
                    pf8[r8] = (sc, sc_evs)
                wfree[sl % 2] = (pe, pe.n)
                B.wait("sync", sc, sc_evs)
                dst = d_sscr[sb * SBT:(sb + 1) * SBT, sl * 256:(sl + 1) * 256] \
                    .rearrange("(tl p) f -> p tl f", p=128)
                src = scol.rearrange("p (tl f) -> p tl f", f=256)
                s_wr[gs] = SP.issue("sync", lambda e, d=dst, s=src:
                                    e.dma_start(out=d, in_=s))
            wo_done_pe = pe.n

        if DEBUG:
            B.wait("sync", ve, ve.n)
            B.wait("sync", sc, sc.n)
            SP.issue("sync", lambda e: e.dma_start(out=d_dbg_xnT, in_=xnT[:, :]))
            SP.issue("sync", lambda e: e.dma_start(out=d_dbg_kv, in_=kv))
            SP.issue("sync", lambda e: e.dma_start(out=d_dbg_carry, in_=carry[:, :]))
            SP.issue("sync", lambda e: e.dma_start(out=d_dbg_tot, in_=totA[:, :]))

        B.barrier(allsems)

        if _PHASE < 5:
            for s_ in SP.sems + GP.sems:
                B.wait("sync", s_, s_.n)
            with nc.Block() as block:
                B.install(block)
            return nc

        hbuf = AB[:, 0:32768]
        wfsl = AB[:, 32768:49152]
        shalf = AB[:, 49152:51200]
        gfin = AB[:, 51200:55296]
        mbuf = AB[:, 55296:57344]
        sdbc = AB[:, 57344:59392]
        xhalf = AFa[:, 0:2048]
        ohalf = AFa[:, 2048:4096]
        t0b = AFa[:, 4096:5120]

        ffree = [None, None, None, None]
        fslot = [0]

        sd_ld = GP.issue("gpsimd", lambda e: e.dma_start(out=sdbc, in_=d_sd))

        chain_idx = [0]
        ch_ev2 = {}; ch_ev3 = {}; ch_ev4 = {}; ch_ev5 = {}
        ch_xld = {}; ch_sld = {}
        tails = []

        def flush_tails():
            for fn in tails:
                fn()
            tails.clear()

        def emit_tail(n, gt, dh):
            if n >= 2:
                B.wait("sync", ve, ch_ev4[n - 2])
                B.wait("sync", ve, ch_ev3[n - 2])
            ch_xld[n] = SP.issue("sync", lambda e, gt=gt, dh=dh, n=n: e.dma_start(
                out=xhalf[:, (n % 2) * 1024:(n % 2) * 1024 + 1024],
                in_=d_x[gt * 128:(gt + 1) * 128, dh * 1024:(dh + 1) * 1024]))
            ch_sld[n] = SP.issue("sync", lambda e, gt=gt, dh=dh, n=n: e.dma_start(
                out=shalf[:, (n % 2) * 1024:(n % 2) * 1024 + 1024],
                in_=d_sscr[gt * 128:(gt + 1) * 128, dh * 1024:(dh + 1) * 1024]))
            B.wait("vector", sc, ch_ev2[n])
            B.waitp("vector", ch_sld[n])
            B.wait("vector", ve, ve.n)
            ch_ev3[n] = B.do("vector", lambda e, n=n: e.tensor_mul(
                mbuf[:, (n % 2) * 1024:(n % 2) * 1024 + 1024],
                gfin[:, (n % 4) * 1024:(n % 4) * 1024 + 1024],
                shalf[:, (n % 2) * 1024:(n % 2) * 1024 + 1024]), ve, 1)
            B.waitp("vector", ch_xld[n])
            if n >= 2:
                B.waitp("vector", ch_ev5[n - 2])
            B.wait("vector", ve, ve.n)
            ch_ev4[n] = B.do("vector", lambda e, n=n: e.tensor_add(
                ohalf[:, (n % 2) * 1024:(n % 2) * 1024 + 1024],
                mbuf[:, (n % 2) * 1024:(n % 2) * 1024 + 1024],
                xhalf[:, (n % 2) * 1024:(n % 2) * 1024 + 1024]), ve, 1)
            B.wait("sync", ve, ch_ev4[n])
            ch_ev5[n] = SP.issue("sync", lambda e, gt=gt, dh=dh, n=n: e.dma_start(
                out=d_out[gt * 128:(gt + 1) * 128, dh * 1024:(dh + 1) * 1024],
                in_=ohalf[:, (n % 2) * 1024:(n % 2) * 1024 + 1024]))

        for tb in range(4):
            up_last_ve = None
            hev_gate = [None] * NIC
            for mat in range(2):
                dram = d_wgT if mat == 0 else d_wuT
                scl = sgb if mat == 0 else sub

                def gate_ldwfx(sl, dram=dram):
                    b = fslot[0] % 4; fslot[0] += 1
                    B.waitp("gpsimd", ffree[b])
                    src = dram[sl, :, :]
                    dst = wfsl[:, b * 4096:(b + 1) * 4096]
                    wld = GP.issue("gpsimd", lambda e, s=src, d=dst:
                                   e.dma_start(out=d, in_=s))
                    B.waitp("vector", wld)
                    B.waitp("vector", ffree[b])
                    B.wait("vector", ve, ve.n)
                    wfx = B.do("vector", lambda e, b=b: e.tensor_scalar_add(
                        wfsl[:, b * 4096:(b + 1) * 4096],
                        wfsl[:, b * 4096:(b + 1) * 4096], -128.0), ve, 1)
                    return b, wfx

                cur = gate_ldwfx(0)
                for sl in range(32):
                    b, wfx = cur
                    cur = gate_ldwfx(sl + 1) if sl + 1 < 32 else None
                    if mat == 0 and sl == 2:
                        flush_tails()
                    last_mm = None
                    for j in range(2):
                        ic = sl * 2 + j
                        r8 = ring8[0] % 8; ring8[0] += 1
                        pview = P[r8 // 2][:, (r8 % 2) * 512:(r8 % 2) * 512 + TB]
                        B.wait("tensor", ve, wfx)
                        B.waitp("tensor", pf8[r8])
                        for dc in range(ND):
                            lhs = wfsl[:, b * 4096 + dc * 256 + j * 128:
                                       b * 4096 + dc * 256 + j * 128 + 128]
                            rhs = xnT[:, dc * T + tb * TB: dc * T + tb * TB + TB]
                            B.do("tensor", lambda e, o=pview, l=lhs, r=rhs,
                                 st=(dc == 0), sp=(dc == ND - 1):
                                 e.matmul(o, l, r, start=st, stop=sp))
                        g_pe = B.inc_last("tensor", pe, 1)
                        last_mm = g_pe
                        if mat == 0:
                            B.wait("scalar", pe, g_pe)
                            B.wait("scalar", sc, sc.n)
                            hev = B.do("scalar", lambda e, ic=ic, pview=pview, scl=scl:
                                       e.activation(hbuf[:, ic * TB:(ic + 1) * TB],
                                                    pview, AF.Silu,
                                                    scale=scl[:, ic:ic + 1]), sc, 1)
                            pf8[r8] = (sc, hev)
                            hev_gate[ic] = hev
                        else:
                            B.wait("vector", pe, g_pe)
                            B.wait("vector", sc, hev_gate[ic])
                            B.wait("vector", ve, ve.n)
                            hev = B.do("vector", lambda e, ic=ic, pview=pview, scl=scl:
                                       e.scalar_tensor_tensor(
                                           hbuf[:, ic * TB:(ic + 1) * TB],
                                           pview, scl[:, ic:ic + 1],
                                           hbuf[:, ic * TB:(ic + 1) * TB],
                                           ALU.mult, ALU.mult), ve, 1)
                            pf8[r8] = (ve, hev)
                            up_last_ve = hev
                    ffree[b] = (pe, last_mm)

            B.wait("tensor", ve, up_last_ve)
            for dh in range(2):
                for r8_ in range(8):
                    B.waitp("tensor", pf8[r8_])

                def down_ldwfx(sl, dh=dh):
                    b = fslot[0] % 4; fslot[0] += 1
                    B.waitp("gpsimd", ffree[b])
                    src = d_wdT[dh, sl, :, :]
                    dst = wfsl[:, b * 4096:(b + 1) * 4096]
                    wld = GP.issue("gpsimd", lambda e, s=src, d=dst:
                                   e.dma_start(out=d, in_=s))
                    B.waitp("vector", wld)
                    B.waitp("vector", ffree[b])
                    B.wait("vector", ve, ve.n)
                    wfx = B.do("vector", lambda e, b=b: e.tensor_scalar_add(
                        wfsl[:, b * 4096:(b + 1) * 4096],
                        wfsl[:, b * 4096:(b + 1) * 4096], -128.0), ve, 1)
                    return b, wfx

                stop_pe = None
                curd = down_ldwfx(0)
                for sl in range(16):
                    b, wfx = curd
                    curd = down_ldwfx(sl + 1) if sl + 1 < 16 else None
                    B.wait("tensor", ve, wfx)
                    if sl == 2:
                        flush_tails()
                    for c4 in range(4):
                        for tile in range(4):
                            lhs = hbuf[:, (sl * 4 + c4) * TB + tile * 128:
                                       (sl * 4 + c4) * TB + tile * 128 + 128]
                            for h in range(2):
                                rhs = wfsl[:, b * 4096 + c4 * 1024 + h * 512:
                                           b * 4096 + c4 * 1024 + h * 512 + 512]
                                st = (sl == 0 and c4 == 0)
                                sp = (sl == 15 and c4 == 3)
                                B.do("tensor", lambda e,
                                     o=P[tile][:, h * 512:(h + 1) * 512],
                                     l=lhs, r=rhs, st=st, sp=sp:
                                     e.matmul(o, l, r, start=st, stop=sp))
                    slot_pe = B.inc_last("tensor", pe, 1)
                    ffree[b] = (pe, slot_pe)
                    if sl == 15:
                        stop_pe = slot_pe

                for tile in range(4):
                    n = chain_idx[0]; chain_idx[0] += 1
                    gt = tb * 4 + tile
                    B.wait("vector", pe, stop_pe)
                    B.waitp("vector", sd_ld)
                    B.wait("vector", ve, ve.n)
                    ev1 = B.do("vector", lambda e, tile=tile, dh=dh: e.tensor_mul(
                        P[tile][:, 0:1024], P[tile][:, 0:1024],
                        sdbc[:, dh * 1024:(dh + 1) * 1024]), ve, 1)
                    B.wait("scalar", ve, ev1)
                    B.wait("scalar", sc, sc.n)
                    if n >= 4:
                        B.wait("scalar", ve, ch_ev3[n - 4])
                    ch_ev2[n] = B.do("scalar", lambda e, tile=tile, n=n: e.activation(
                        gfin[:, (n % 4) * 1024:(n % 4) * 1024 + 1024],
                        P[tile][:, 0:1024], AF.Silu), sc, 1)
                    pf8[2 * tile] = (sc, ch_ev2[n])
                    pf8[2 * tile + 1] = (sc, ch_ev2[n])
                    tails.append(lambda n=n, gt=gt, dh=dh: emit_tail(n, gt, dh))

        flush_tails()

        for s in SP.sems + GP.sems:
            B.wait("sync", s, s.n)

        with nc.Block() as block:
            B.install(block)

    return nc


def _host_prep(L0_attn_x, L0_attn_Wq, L0_attn_Wk, L0_attn_Wv, L0_attn_Wo,
               L0_ffn_gate_w, L0_ffn_gate_scale, L0_ffn_up_w, L0_ffn_up_scale,
               L0_ffn_down_w, L0_ffn_down_scale):
    x = np.asarray(L0_attn_x, dtype=np.float32)
    def tile_attn128(w):
        wt = np.asarray(w, dtype=np.float32).T.reshape(16, 128, 16, 128)
        return np.ascontiguousarray(wt.transpose(2, 1, 0, 3).reshape(16, 128, 2048))

    def tile_attn256(w):
        wt = np.asarray(w, dtype=np.float32).T.reshape(16, 128, 8, 256)
        return np.ascontiguousarray(wt.transpose(2, 1, 0, 3).reshape(8, 128, 4096))

    wqT = tile_attn128(L0_attn_Wq)
    wkT = tile_attn128(L0_attn_Wk)
    wvT = tile_attn128(L0_attn_Wv)
    woT = tile_attn256(L0_attn_Wo)

    def tile_gate(w):
        wt = np.asarray(w).astype(np.uint8).T.reshape(16, 128, 32, 256)
        return np.ascontiguousarray(wt.transpose(2, 1, 0, 3).reshape(32, 128, 4096))

    wgT = tile_gate(L0_ffn_gate_w)
    wuT = tile_gate(L0_ffn_up_w)
    wdt = np.asarray(L0_ffn_down_w).astype(np.uint8).T.reshape(16, 4, 128, 2, 1024)
    wdT = np.ascontiguousarray(wdt.transpose(3, 0, 2, 1, 4).reshape(2, 16, 128, 4096))
    sg = np.ascontiguousarray(np.asarray(L0_ffn_gate_scale, dtype=np.float32)
                              .reshape(NIC, 128).T)
    su = np.ascontiguousarray(np.asarray(L0_ffn_up_scale, dtype=np.float32)
                              .reshape(NIC, 128).T)
    sd = np.ascontiguousarray(np.broadcast_to(
        np.asarray(L0_ffn_down_scale, dtype=np.float32), (128, D)))

    in_maps = []
    for c in range(8):
        b, j = c // 4, c % 4
        xs = np.ascontiguousarray(x[b, j * T:(j + 1) * T, :])
        mvec = np.zeros(8, dtype=np.float32)
        for r in range(8):
            if r // 4 == b and r % 4 < j:
                mvec[r] = 1.0
        mask = np.ascontiguousarray(np.broadcast_to(mvec, (128, 8)))
        in_maps.append(dict(x=xs, wqT=wqT, wkT=wkT, wvT=wvT, woT=woT,
                            wgT=wgT, wuT=wuT, wdT=wdT, sg=sg, su=su, sd=sd,
                            mask=mask))
    return in_maps


def kernel(**inputs):
    in_maps = _host_prep(**inputs)
    nc = _build_nc()
    res = run_bass_kernel_spmd(nc, in_maps, core_ids=list(range(8)))
    if DEBUG:
        kernel.last_results = res
    full = np.empty((2, 8192, D), dtype=np.float32)
    for c in range(8):
        b, j = c // 4, c % 4
        full[b, j * T:(j + 1) * T, :] = np.asarray(res.results[c]["out"],
                                                   dtype=np.float32)
    return full


# version 17
# speedup vs baseline: 1.0319x; 1.0263x over previous
import numpy as np
from contextlib import ExitStack
import concourse.bass as bass
import concourse.mybir as mybir
from concourse.bass_utils import run_bass_kernel_spmd

F32 = mybir.dt.float32
BF16 = mybir.dt.bfloat16
U8 = mybir.dt.uint8
AF = mybir.ActivationFunctionType
ALU = mybir.AluOpType

D = 2048
II = 8192
T = 2048
NT = 16
NE = 16
ND = 16
NIC = 64
SBT = 1024
TB = 512
DENOM = float(np.sqrt(np.float32(2048.0)) / np.float32(8192.0))
EPS = 1e-5

DEBUG = False
_NO_CC = False
_PHASE = 5


class Sem:
    __slots__ = ("h", "n")

    def __init__(self, h):
        self.h = h
        self.n = 0


class Build:
    ENGS = ("sync", "gpsimd", "scalar", "vector", "tensor")

    def __init__(self):
        self.ops = {e: [] for e in self.ENGS}
        self._waited = {e: {} for e in self.ENGS}

    def wait(self, eng, sem, val):
        if val is None or val <= 0:
            return
        w = self._waited[eng]
        if w.get(id(sem), -1) >= val:
            return
        w[id(sem)] = val
        self.ops[eng].append(lambda e, h=sem.h, v=val: e.wait_ge(h, v))

    def waitp(self, eng, pair):
        if pair is not None:
            self.wait(eng, pair[0], pair[1])

    def do(self, eng, fn, sem=None, k=1):
        if sem is None:
            self.ops[eng].append(fn)
            return None
        sem.n += k
        tgt = sem.n
        self.ops[eng].append(lambda e, f=fn, h=sem.h, kk=k: f(e).then_inc(h, kk))
        return tgt

    def inc_last(self, eng, sem, k=1):
        sem.n += k
        tgt = sem.n
        prev = self.ops[eng].pop()
        self.ops[eng].append(lambda e, p=prev, h=sem.h, kk=k: p(e).then_inc(h, kk))
        return tgt

    def barrier(self, sems):
        snap = [(s, s.n) for s in sems]
        for eng in self.ENGS:
            for s, v in snap:
                self.wait(eng, s, v)

    def install(self, block):
        for name in self.ENGS:
            ops = self.ops[name]

            def body(e, ops=ops):
                for fn in ops:
                    fn(e)

            getattr(block, name)(body)


class DmaPool:

    def __init__(self, B, sems):
        self.B = B
        self.sems = sems
        self.i = 0

    def issue(self, eng, fn):
        s = self.sems[self.i % len(self.sems)]
        self.i += 1
        self.B.wait(eng, s, s.n)
        tgt = self.B.do(eng, fn, s, 16)
        return (s, tgt)


def _build_nc():
    nc = bass.Bass("TRN2", target_bir_lowering=False, detect_race_conditions=False)

    d_x = nc.dram_tensor("x", [T, D], F32, kind="ExternalInput").ap()
    d_wqT = nc.dram_tensor("wqT", [16, 128, 2048], F32, kind="ExternalInput").ap()
    d_wkT = nc.dram_tensor("wkT", [16, 128, 2048], F32, kind="ExternalInput").ap()
    d_wvT = nc.dram_tensor("wvT", [16, 128, 2048], F32, kind="ExternalInput").ap()
    d_woT = nc.dram_tensor("woT", [8, 128, 4096], F32, kind="ExternalInput").ap()
    d_wgT = nc.dram_tensor("wgT", [32, 128, 4096], U8, kind="ExternalInput").ap()
    d_wuT = nc.dram_tensor("wuT", [32, 128, 4096], U8, kind="ExternalInput").ap()
    d_wdT = nc.dram_tensor("wdT", [2, 16, 128, 4096], U8, kind="ExternalInput").ap()
    d_sg = nc.dram_tensor("sg", [128, NIC], F32, kind="ExternalInput").ap()
    d_su = nc.dram_tensor("su", [128, NIC], F32, kind="ExternalInput").ap()
    d_sd = nc.dram_tensor("sd", [128, D], F32, kind="ExternalInput").ap()
    d_mask = nc.dram_tensor("mask", [128, 8], F32, kind="ExternalInput").ap()
    d_out = nc.dram_tensor("out", [T, D], F32, kind="ExternalOutput").ap()

    s_kind = "ExternalOutput" if DEBUG else "Internal"
    d_sscr = nc.dram_tensor("s_scr", [T, D], BF16, kind=s_kind).ap()
    t_ccin = nc.dram_tensor("cc_in", [128, NE], F32)
    t_ccout = nc.dram_tensor("cc_out", [128 * 8, NE], F32)
    if DEBUG:
        d_dbg_xnT = nc.dram_tensor("dbg_xnT", [128, NE * T], BF16, kind="ExternalOutput").ap()
        d_dbg_kv = nc.dram_tensor("dbg_kv", [128, NE * T], BF16, kind="ExternalOutput").ap()
        d_dbg_carry = nc.dram_tensor("dbg_carry", [128, NE], F32, kind="ExternalOutput").ap()
        d_dbg_tot = nc.dram_tensor("dbg_tot", [128, NE], F32, kind="ExternalOutput").ap()
        d_dbg_q0 = nc.dram_tensor("dbg_q0", [128, NE * SBT], BF16, kind="ExternalOutput").ap()

    B = Build()
    es = ExitStack()
    with es:
        E = es.enter_context
        pe = Sem(E(nc.semaphore("pe")))
        sc = Sem(E(nc.semaphore("sc")))
        ve = Sem(E(nc.semaphore("ve")))
        gp = Sem(E(nc.semaphore("gp")))
        cc = Sem(E(nc.semaphore("cc")))
        SP = DmaPool(B, [Sem(E(nc.semaphore(f"dp{i}"))) for i in range(12)])
        GP = DmaPool(B, [Sem(E(nc.semaphore(f"dq{i}"))) for i in range(12)])
        allsems = [pe, sc, ve, gp] + SP.sems + GP.sems

        ident = E(nc.sbuf_tensor("ident", [128, 128], BF16))
        zeros = E(nc.sbuf_tensor("zeros", [128, SBT], BF16))
        sgb = E(nc.sbuf_tensor("sgb", [128, NIC], F32))
        sub = E(nc.sbuf_tensor("sub", [128, NIC], F32))
        maskb = E(nc.sbuf_tensor("maskb", [128, 8], F32))
        tot0 = E(nc.sbuf_tensor("tot0", [128, NE], F32))
        tot1 = E(nc.sbuf_tensor("tot1", [128, NE], F32))
        totA = E(nc.sbuf_tensor("totA", [128, NE], F32))
        gath = E(nc.sbuf_tensor("gath", [128, 8 * NE], F32))
        carry = E(nc.sbuf_tensor("carry", [128, NE], F32))
        carry2 = E(nc.sbuf_tensor("carry2", [128, NE], F32))
        stats = E(nc.sbuf_tensor("stats", [128, 16], F32))
        xnT = E(nc.sbuf_tensor("xnT", [128, NE * T], BF16))

        AB = E(nc.sbuf_tensor("AB", [128, 59392], BF16))
        AFa = E(nc.sbuf_tensor("AF", [128, 5120], F32))
        P = [E(nc.psum_tensor(f"P{i}", [128, 1024], F32)) for i in range(4)]

        xln = AFa[:, 0:4096]
        sqscr = AB[:, 0:2048]

        x_ld = [None] * NT
        ln_vA = [None] * NT
        ln_vC = [None] * NT
        ln_scD = [None] * NT
        tr_pe = [[None, None] for _ in range(NT)]
        tr_ve = [[None, None] for _ in range(NT)]

        xnT_r = xnT[:, :].rearrange("p (c n) -> p c n", n=T)

        kv = AB[:, 0:32768]
        qv = AB[:, 32768:49152]
        wsl = AB[:, 49152:57344]
        scol = AB[:, 57344:59392]
        ksb = AB[:, 32768:34816]
        kvt = AFa[:, 2048:3072]
        pf8 = [None] * 8
        wfree = [None, None]
        wqfree = [None, None, None, None]
        ring = [0]
        ring8 = [0]

        def pwait_full(eng, r):
            B.waitp(eng, pf8[2 * r])
            B.waitp(eng, pf8[2 * r + 1])

        def pset_full(r, pair):
            pf8[2 * r] = pair
            pf8[2 * r + 1] = pair
        stt_tgt = {}
        kvseq = [0]
        kvld = [0]
        kvA = [0]
        pend_kv = []
        xnT_sb0_ve = [None]

        def load_wslice(b, dram, sl, pe_free):
            B.waitp("gpsimd", pe_free)
            src = dram[sl, :, :]
            dst = wsl[:, b * 4096:(b + 1) * 4096]
            return GP.issue("gpsimd", lambda e, s=src, d=dst: e.dma_start(out=d, in_=s))

        def load_wslot(slot, dram, sl, pe_free):
            B.waitp("gpsimd", pe_free)
            src = dram[sl, :, :]
            dst = wsl[:, slot * 2048:(slot + 1) * 2048]
            return GP.issue("gpsimd", lambda e, s=src, d=dst: e.dma_start(out=d, in_=s))

        def kv_load(ec):
            seq = kvld[0]; kvld[0] += 1
            kslot = seq % 2
            vslot = 2 + seq % 2
            wk_rdy = load_wslot(kslot, d_wkT, ec, wqfree[kslot])
            wv_rdy = load_wslot(vslot, d_wvT, ec, wqfree[vslot])
            return [kslot, vslot, wk_rdy, wv_rdy]

        def emit_kv_mms(ec, sb, r0, r1, ld):
            seq = kvseq[0]; kvseq[0] += 1
            kslot, vslot, wk_rdy, wv_rdy = ld
            B.waitp("tensor", wk_rdy)
            B.waitp("tensor", wv_rdy)
            pwait_full("tensor", r0)
            if sb == 0:
                B.waitp("tensor", xnT_sb0_ve[0])
            for dc in range(ND):
                lhs = wsl[:, kslot * 2048 + dc * 128: kslot * 2048 + dc * 128 + 128]
                for h in range(2):
                    rhs = xnT[:, dc * T + sb * SBT + h * 512:
                              dc * T + sb * SBT + h * 512 + 512]
                    B.do("tensor", lambda e, o=P[r0][:, h * 512:(h + 1) * 512],
                         l=lhs, r=rhs, st=(dc == 0), sp=(dc == ND - 1):
                         e.matmul(o, l, r, start=st, stop=sp))
            k_pe = B.inc_last("tensor", pe, 1)
            pwait_full("tensor", r1)
            for dc in range(ND):
                lhs = wsl[:, vslot * 2048 + dc * 128: vslot * 2048 + dc * 128 + 128]
                for h in range(2):
                    rhs = xnT[:, dc * T + sb * SBT + h * 512:
                              dc * T + sb * SBT + h * 512 + 512]
                    B.do("tensor", lambda e, o=P[r1][:, h * 512:(h + 1) * 512],
                         l=lhs, r=rhs, st=(dc == 0), sp=(dc == ND - 1):
                         e.matmul(o, l, r, start=st, stop=sp))
            v_pe = B.inc_last("tensor", pe, 1)
            return dict(seq=seq, ec=ec, sb=sb, r0=r0, r1=r1,
                        k_pe=k_pe, v_pe=v_pe)

        def emit_kv_epi(st):
            seq, ec, sb = st["seq"], st["ec"], st["sb"]
            r0, r1 = st["r0"], st["r1"]
            koff = ec * T + sb * SBT
            kview = ksb[:, (seq % 2) * 1024:(seq % 2) * 1024 + 1024]
            B.wait("scalar", pe, st["k_pe"])
            B.wait("scalar", sc, sc.n)
            if seq >= 2:
                B.wait("scalar", ve, stt_tgt[seq - 2])
            kev = B.do("scalar", lambda e, kv_=kview, r0=r0:
                       e.copy(kv_, P[r0][:, :]), sc, 1)
            pset_full(r0, (sc, kev))

            B.wait("vector", pe, st["v_pe"])
            B.wait("vector", sc, kev)
            B.wait("vector", ve, ve.n)
            tt = tot0 if sb == 0 else tot1
            stt = B.do("vector", lambda e, kv_=kview, r1=r1, koff=koff,
                       tt=tt, ec=ec: e.scalar_tensor_tensor(
                           kv[:, koff:koff + SBT], kv_, 1.0, P[r1][:, :],
                           ALU.mult, ALU.mult,
                           accum_out=tt[:, ec:ec + 1]), ve, 1)
            pset_full(r1, (ve, stt))
            stt_tgt[seq] = stt

        def vstep(fn):
            B.wait("vector", ve, ve.n)
            return B.do("vector", fn, ve, 1)

        def ln_views(t):
            xb = xln[:, (t % 2) * D:(t % 2) * D + D]
            xnbf = AB[:, 2048 + (t % 2) * 2048: 2048 + (t % 2) * 2048 + 2048]
            sb8 = (t % 2) * 8
            st = [stats[:, sb8 + i:sb8 + i + 1] for i in range(8)]
            return xb, xnbf, st

        def ln_stage1(t):
            xb, xnbf, st = ln_views(t)
            eng, pool = ("sync", SP) if t % 2 == 0 else ("gpsimd", GP)
            if t >= 2:
                B.wait(eng, sc, ln_scD[t - 2])
            x_ld[t] = pool.issue(eng, lambda e, t=t, xb=xb: e.dma_start(
                out=xb, in_=d_x[t * 128:(t + 1) * 128, :]))

        def ln_stageA(t):
            xb, xnbf, st = ln_views(t)
            s1, s2, mcol, t2c, varc = st[0], st[1], st[2], st[3], st[4]
            B.waitp("vector", x_ld[t])
            if t >= 2:
                B.wait("vector", sc, ln_scD[t - 2])
            vstep(lambda e, xb=xb, s2=s2: e.scalar_tensor_tensor(
                sqscr, xb, 1.0, xb, ALU.mult, ALU.mult, accum_out=s2))
            vstep(lambda e, xb=xb, s1=s1: e.reduce_sum(s1, xb, axis=mybir.AxisListType.X))
            vstep(lambda e, mcol=mcol, s1=s1: e.tensor_scalar_mul(mcol, s1, 1.0 / D))
            vstep(lambda e, t2c=t2c, s2=s2: e.tensor_scalar_mul(t2c, s2, 1.0 / D))
            vstep(lambda e, varc=varc, mcol=mcol: e.tensor_mul(varc, mcol, mcol))
            vstep(lambda e, varc=varc, t2c=t2c: e.tensor_sub(varc, t2c, varc))
            ln_vA[t] = vstep(lambda e, varc=varc: e.tensor_scalar_add(varc, varc, EPS))
            B.wait("scalar", ve, ln_vA[t])
            B.wait("scalar", sc, sc.n)
            ln_sB[t] = B.do("scalar", lambda e, sdc=st[5], varc=varc:
                            e.sqrt(sdc, varc), sc, 1)

        def ln_stageB(t):
            xb, xnbf, st = ln_views(t)
            mcol, sdc, rstd, nbc = st[2], st[5], st[6], st[7]
            B.wait("vector", sc, ln_sB[t])
            vstep(lambda e, rstd=rstd, sdc=sdc: e.reciprocal(rstd, sdc))
            ln_vC[t] = vstep(lambda e, nbc=nbc, mcol=mcol, rstd=rstd: e.scalar_tensor_tensor(
                nbc, mcol, -1.0, rstd, ALU.mult, ALU.mult))

            B.wait("scalar", ve, ln_vC[t])
            B.wait("scalar", sc, sc.n)
            if t >= 2:
                B.wait("scalar", pe, tr_pe[t - 2][1])
            ln_scD[t] = B.do("scalar", lambda e, xb=xb, xnbf=xnbf, nbc=nbc,
                             rstd=rstd: e.activation(
                xnbf, xb, AF.Identity, bias=nbc, scale=rstd), sc, 1)

            if pend_kv:
                emit_kv_epi(pend_kv.pop(0))

            B.wait("tensor", sc, ln_scD[t])
            B.wait("tensor", gp, ident_rdy)
            for g in range(2):
                if t >= 1:
                    B.waitp("tensor", tr_ve[t - 1][g])
                for j in range(8):
                    c = g * 8 + j
                    B.do("tensor", lambda e, g=g, j=j, c=c, xnbf=xnbf: e.matmul(
                        P[g][:, j * 128:(j + 1) * 128],
                        xnbf[:, c * 128:(c + 1) * 128], ident[:],
                        start=True, stop=True))
                tr_pe[t][g] = B.inc_last("tensor", pe, 1)

                B.wait("scalar", pe, tr_pe[t][g])
                B.wait("scalar", sc, sc.n)
                out_ap = xnT_r[:, g * 8:(g + 1) * 8, t * 128:(t + 1) * 128]
                in_ap = P[g][:, 0:1024].rearrange("p (j n) -> p j n", n=128)
                tr_ve[t][g] = (sc, B.do("scalar", lambda e, o=out_ap, i=in_ap:
                                        e.copy(o, i), sc, 1))

            if t == 7:
                xnT_sb0_ve[0] = tr_ve[7][1]
            if t >= 8 and _PHASE >= 2:
                ld = kv_load(8 + kvA[0])
                st_ = emit_kv_mms(8 + kvA[0], 0, 2, 3, ld)
                wqfree[ld[0]] = (pe, st_["v_pe"])
                wqfree[ld[1]] = (pe, st_["v_pe"])
                pend_kv.append(st_)
                kvA[0] += 1

        ln_sB = [None] * NT
        ln_stage1(0)
        ln_stage1(1)
        zeros_rdy = B.do("gpsimd", lambda e: e.memset(zeros[:], 0.0), gp, 1)
        ms2 = B.do("gpsimd", lambda e: e.memset(ident[:], 0.0), gp, 1)
        B.wait("gpsimd", gp, ms2)
        ident_rdy = B.do("gpsimd", lambda e: e.affine_select(
            out=ident[:], in_=ident[:], compare_op=ALU.not_equal,
            fill=1.0, base=0, pattern=[[-1, 128]], channel_multiplier=1), gp, 1)
        ln_stageA(0)
        for t in range(NT):
            ln_stageB(t)
            if t + 2 < NT:
                ln_stage1(t + 2)
            if t + 1 < NT:
                ln_stageA(t + 1)
        sg_ld = SP.issue("sync", lambda e: e.dma_start(out=sgb[:], in_=d_sg))
        su_ld = SP.issue("sync", lambda e: e.dma_start(out=sub[:], in_=d_su))
        mask_ld = SP.issue("sync", lambda e: e.dma_start(out=maskb[:], in_=d_mask))

        if _PHASE < 2:
            for s_ in SP.sems + GP.sems:
                B.wait("sync", s_, s_.n)
            with nc.Block() as block:
                B.install(block)
            return nc

        while pend_kv:
            emit_kv_epi(pend_kv.pop(0))

        pset_full(0, tr_ve[NT - 1][0])
        pset_full(1, tr_ve[NT - 1][1])
        B.waitp("tensor", tr_ve[NT - 1][1])
        for ec in range(8):
            ld = kv_load(ec)
            st0 = emit_kv_mms(ec, 0, 0, 1, ld)
            emit_kv_epi(st0)
            st1 = emit_kv_mms(ec, 1, 2, 3, ld)
            wqfree[ld[0]] = (pe, st1["v_pe"])
            wqfree[ld[1]] = (pe, st1["v_pe"])
            emit_kv_epi(st1)
        for i, ec in enumerate(range(8, NE)):
            ld = kv_load(ec)
            r0, r1 = (0, 1) if i % 2 == 0 else (2, 3)
            st_ = emit_kv_mms(ec, 1, r0, r1, ld)
            wqfree[ld[0]] = (pe, st_["v_pe"])
            wqfree[ld[1]] = (pe, st_["v_pe"])
            emit_kv_epi(st_)
        wfree[0] = (pe, pe.n)
        wfree[1] = (pe, pe.n)

        if _PHASE < 3:
            for s_ in SP.sems + GP.sems:
                B.wait("sync", s_, s_.n)
            B.barrier(allsems)
            with nc.Block() as block:
                B.install(block)
            return nc

        B.wait("vector", ve, ve.n)
        totA_ve = B.do("vector", lambda e: e.tensor_add(totA[:], tot0[:], tot1[:]), ve, 1)
        B.wait("sync", ve, totA_ve)
        ccw = SP.issue("sync", lambda e: e.dma_start(out=t_ccin.ap(), in_=totA[:]))
        B.waitp("gpsimd", ccw)
        if not _NO_CC:
            B.do("gpsimd", lambda e: e.collective_compute(
                "AllGather", ALU.bypass, replica_groups=[list(range(8))],
                ins=[t_ccin.ap().opt()], outs=[t_ccout.ap().opt()]), cc, 1)
        carry_ve = None

        if _PHASE < 4:
            for s_ in SP.sems + GP.sems:
                B.wait("sync", s_, s_.n)
            B.barrier(allsems)
            with nc.Block() as block:
                B.install(block)
            return nc

        s_wr = [None] * 16
        wo_done_pe = None
        for sb in range(2):
            wqfree[0] = wqfree[1] = wfree[0]
            wqfree[2] = wqfree[3] = wfree[1]
            qv_sc = [None] * NE
            for sl in range(NE):
                slot = sl % 4
                w_rdy = load_wslot(slot, d_wqT, sl, wqfree[slot])
                ec = sl
                r0 = ring[0] % 4; ring[0] += 1
                B.waitp("tensor", w_rdy)
                pwait_full("tensor", r0)
                for dc in range(ND):
                    lhs = wsl[:, slot * 2048 + dc * 128: slot * 2048 + dc * 128 + 128]
                    for h in range(2):
                        rhs = xnT[:, dc * T + sb * SBT + h * 512:
                                  dc * T + sb * SBT + h * 512 + 512]
                        B.do("tensor", lambda e, o=P[r0][:, h * 512:(h + 1) * 512],
                             l=lhs, r=rhs, st=(dc == 0), sp=(dc == ND - 1):
                             e.matmul(o, l, r, start=st, stop=sp))
                q_pe = B.inc_last("tensor", pe, 1)
                wqfree[slot] = (pe, q_pe)
                B.wait("scalar", pe, q_pe)
                B.wait("scalar", sc, sc.n)
                if ec < 2:
                    B.wait("scalar", ve, max(stt_tgt.values()))
                if sb == 1:
                    B.wait("scalar", pe, wo_done_pe)
                qv_sc[ec] = B.do("scalar", lambda e, ec=ec, r0=r0: e.activation(
                    qv[:, ec * SBT:(ec + 1) * SBT], P[r0][:, :],
                    AF.Identity, scale=DENOM), sc, 1)
                pset_full(r0, (sc, qv_sc[ec]))
            wfree[0] = (pe, pe.n)
            wfree[1] = (pe, pe.n)

            if DEBUG and sb == 0:
                B.wait("sync", sc, qv_sc[NE - 1])
                SP.issue("sync", lambda e: e.dma_start(out=d_dbg_q0, in_=qv))

            if sb == 0:
                B.wait("sync", cc, 1)
                g_lds = []
                for r in range(8):
                    g_lds.append(SP.issue("sync", lambda e, r=r: e.dma_start(
                        out=gath[:, r * NE:(r + 1) * NE],
                        in_=t_ccout.ap()[r * 128:(r + 1) * 128, :])))
                for t_ in g_lds:
                    B.waitp("vector", t_)
                B.waitp("vector", mask_ld)
                B.wait("vector", ve, ve.n)
                B.do("vector", lambda e: e.tensor_scalar(
                    carry[:], gath[:, 0:NE], maskb[:, 0:1], None, ALU.mult), ve, 1)
                for r in range(1, 8):
                    B.wait("vector", ve, ve.n)
                    carry_ve = B.do("vector", lambda e, r=r: e.scalar_tensor_tensor(
                        carry[:], gath[:, r * NE:(r + 1) * NE], maskb[:, r:r + 1],
                        carry[:], ALU.mult, ALU.add), ve, 1)

            vhat_ve = None
            B.wait("vector", gp, zeros_rdy)
            B.wait("vector", ve, max(stt_tgt.values()))
            if carry_ve:
                B.wait("vector", ve, carry_ve)
            for ec in range(NE):
                koff = ec * T + sb * SBT
                init = carry[:, ec:ec + 1] if sb == 0 else carry2[:, ec:ec + 1]
                B.wait("vector", sc, qv_sc[ec])
                B.wait("vector", ve, ve.n)
                B.do("vector", lambda e, koff=koff, init=init: e.tensor_tensor_scan(
                    kvt, kv[:, koff:koff + SBT], zeros[:, :], init,
                    ALU.add, ALU.add), ve, 1)
                if sb == 0:
                    B.wait("vector", ve, ve.n)
                    B.do("vector", lambda e, ec=ec: e.tensor_copy(
                        carry2[:, ec:ec + 1], kvt[:, SBT - 1:SBT]), ve, 1)
                B.wait("vector", ve, ve.n)
                vhat_ve = B.do("vector", lambda e, ec=ec: e.tensor_mul(
                    qv[:, ec * SBT:(ec + 1) * SBT],
                    qv[:, ec * SBT:(ec + 1) * SBT], kvt), ve, 1)

            B.wait("tensor", ve, vhat_ve)
            for sl in range(8):
                w_rdy = load_wslice(sl % 2, d_woT, sl, wfree[sl % 2])
                sc_evs = None
                gs = sb * 8 + sl
                for tile in range(8):
                    r8 = ring8[0] % 8; ring8[0] += 1
                    pview = P[r8 // 2][:, (r8 % 2) * 512:(r8 % 2) * 512 + 256]
                    B.waitp("tensor", w_rdy)
                    B.waitp("tensor", pf8[r8])
                    for ec in range(NE):
                        lhs = qv[:, ec * SBT + tile * 128: ec * SBT + tile * 128 + 128]
                        rhs = wsl[:, (sl % 2) * 4096 + ec * 256: (sl % 2) * 4096 + ec * 256 + 256]
                        B.do("tensor", lambda e, o=pview, l=lhs, r=rhs,
                             st=(ec == 0), sp=(ec == NE - 1):
                             e.matmul(o, l, r, start=st, stop=sp))
                    wo_pe = B.inc_last("tensor", pe, 1)
                    B.wait("scalar", pe, wo_pe)
                    B.wait("scalar", sc, sc.n)
                    if gs >= 1 and tile == 0:
                        B.waitp("scalar", s_wr[gs - 1])
                    sc_evs = B.do("scalar", lambda e, tile=tile, pview=pview:
                                  e.copy(scol[:, tile * 256: tile * 256 + 256],
                                         pview), sc, 1)
                    pf8[r8] = (sc, sc_evs)
                wfree[sl % 2] = (pe, pe.n)
                B.wait("sync", sc, sc_evs)
                dst = d_sscr[sb * SBT:(sb + 1) * SBT, sl * 256:(sl + 1) * 256] \
                    .rearrange("(tl p) f -> p tl f", p=128)
                src = scol.rearrange("p (tl f) -> p tl f", f=256)
                s_wr[gs] = SP.issue("sync", lambda e, d=dst, s=src:
                                    e.dma_start(out=d, in_=s))
            wo_done_pe = pe.n

        if DEBUG:
            B.wait("sync", ve, ve.n)
            B.wait("sync", sc, sc.n)
            SP.issue("sync", lambda e: e.dma_start(out=d_dbg_xnT, in_=xnT[:, :]))
            SP.issue("sync", lambda e: e.dma_start(out=d_dbg_kv, in_=kv))
            SP.issue("sync", lambda e: e.dma_start(out=d_dbg_carry, in_=carry[:, :]))
            SP.issue("sync", lambda e: e.dma_start(out=d_dbg_tot, in_=totA[:, :]))

        B.barrier(allsems)

        if _PHASE < 5:
            for s_ in SP.sems + GP.sems:
                B.wait("sync", s_, s_.n)
            with nc.Block() as block:
                B.install(block)
            return nc

        hbuf = AB[:, 0:32768]
        wfsl = AB[:, 32768:49152]
        shalf = AB[:, 49152:51200]
        gfin = AB[:, 51200:55296]
        mbuf = AB[:, 55296:57344]
        sdbc = AB[:, 57344:59392]
        xhalf = AFa[:, 0:2048]
        ohalf = AFa[:, 2048:4096]
        t0b = AFa[:, 4096:5120]

        ffree = [None, None, None, None]
        fslot = [0]

        sd_ld = GP.issue("gpsimd", lambda e: e.dma_start(out=sdbc, in_=d_sd))

        chain_idx = [0]
        ch_ev2 = {}; ch_ev3 = {}; ch_ev4 = {}; ch_ev5 = {}
        ch_xld = {}; ch_sld = {}
        tails = []

        def flush_tails():
            for fn in tails:
                fn()
            tails.clear()

        def emit_tail(n, gt, dh):
            if n >= 2:
                B.wait("sync", ve, ch_ev4[n - 2])
                B.wait("sync", ve, ch_ev3[n - 2])
            ch_xld[n] = SP.issue("sync", lambda e, gt=gt, dh=dh, n=n: e.dma_start(
                out=xhalf[:, (n % 2) * 1024:(n % 2) * 1024 + 1024],
                in_=d_x[gt * 128:(gt + 1) * 128, dh * 1024:(dh + 1) * 1024]))
            ch_sld[n] = SP.issue("sync", lambda e, gt=gt, dh=dh, n=n: e.dma_start(
                out=shalf[:, (n % 2) * 1024:(n % 2) * 1024 + 1024],
                in_=d_sscr[gt * 128:(gt + 1) * 128, dh * 1024:(dh + 1) * 1024]))
            B.wait("vector", sc, ch_ev2[n])
            B.waitp("vector", ch_sld[n])
            B.wait("vector", ve, ve.n)
            ch_ev3[n] = B.do("vector", lambda e, n=n: e.tensor_mul(
                mbuf[:, (n % 2) * 1024:(n % 2) * 1024 + 1024],
                gfin[:, (n % 4) * 1024:(n % 4) * 1024 + 1024],
                shalf[:, (n % 2) * 1024:(n % 2) * 1024 + 1024]), ve, 1)
            B.waitp("vector", ch_xld[n])
            if n >= 2:
                B.waitp("vector", ch_ev5[n - 2])
            B.wait("vector", ve, ve.n)
            ch_ev4[n] = B.do("vector", lambda e, n=n: e.tensor_add(
                ohalf[:, (n % 2) * 1024:(n % 2) * 1024 + 1024],
                mbuf[:, (n % 2) * 1024:(n % 2) * 1024 + 1024],
                xhalf[:, (n % 2) * 1024:(n % 2) * 1024 + 1024]), ve, 1)
            B.wait("sync", ve, ch_ev4[n])
            ch_ev5[n] = SP.issue("sync", lambda e, gt=gt, dh=dh, n=n: e.dma_start(
                out=d_out[gt * 128:(gt + 1) * 128, dh * 1024:(dh + 1) * 1024],
                in_=ohalf[:, (n % 2) * 1024:(n % 2) * 1024 + 1024]))

        for tb in range(4):
            up_last_ve = None
            hev_gate = [None] * NIC
            for mat in range(2):
                dram = d_wgT if mat == 0 else d_wuT
                scl = sgb if mat == 0 else sub

                def gate_ldwfx(sl, dram=dram):
                    b = fslot[0] % 4; fslot[0] += 1
                    B.waitp("gpsimd", ffree[b])
                    src = dram[sl, :, :]
                    dst = wfsl[:, b * 4096:(b + 1) * 4096]
                    wld = GP.issue("gpsimd", lambda e, s=src, d=dst:
                                   e.dma_start(out=d, in_=s))
                    B.waitp("vector", wld)
                    B.waitp("vector", ffree[b])
                    B.wait("vector", ve, ve.n)
                    wfx = B.do("vector", lambda e, b=b: e.tensor_scalar_add(
                        wfsl[:, b * 4096:(b + 1) * 4096],
                        wfsl[:, b * 4096:(b + 1) * 4096], -128.0), ve, 1)
                    return b, wfx

                cur = gate_ldwfx(0)
                for sl in range(32):
                    b, wfx = cur
                    cur = gate_ldwfx(sl + 1) if sl + 1 < 32 else None
                    if mat == 0 and sl == 2:
                        flush_tails()
                    last_mm = None
                    for j in range(2):
                        ic = sl * 2 + j
                        r8 = ring8[0] % 8; ring8[0] += 1
                        pview = P[r8 // 2][:, (r8 % 2) * 512:(r8 % 2) * 512 + TB]
                        B.wait("tensor", ve, wfx)
                        B.waitp("tensor", pf8[r8])
                        for dc in range(ND):
                            lhs = wfsl[:, b * 4096 + dc * 256 + j * 128:
                                       b * 4096 + dc * 256 + j * 128 + 128]
                            rhs = xnT[:, dc * T + tb * TB: dc * T + tb * TB + TB]
                            B.do("tensor", lambda e, o=pview, l=lhs, r=rhs,
                                 st=(dc == 0), sp=(dc == ND - 1):
                                 e.matmul(o, l, r, start=st, stop=sp))
                        g_pe = B.inc_last("tensor", pe, 1)
                        last_mm = g_pe
                        if mat == 0:
                            B.wait("scalar", pe, g_pe)
                            B.wait("scalar", sc, sc.n)
                            hev = B.do("scalar", lambda e, ic=ic, pview=pview, scl=scl:
                                       e.activation(hbuf[:, ic * TB:(ic + 1) * TB],
                                                    pview, AF.Silu,
                                                    scale=scl[:, ic:ic + 1]), sc, 1)
                            pf8[r8] = (sc, hev)
                            hev_gate[ic] = hev
                        else:
                            B.wait("vector", pe, g_pe)
                            B.wait("vector", sc, hev_gate[ic])
                            B.wait("vector", ve, ve.n)
                            hev = B.do("vector", lambda e, ic=ic, pview=pview, scl=scl:
                                       e.scalar_tensor_tensor(
                                           hbuf[:, ic * TB:(ic + 1) * TB],
                                           pview, scl[:, ic:ic + 1],
                                           hbuf[:, ic * TB:(ic + 1) * TB],
                                           ALU.mult, ALU.mult), ve, 1)
                            pf8[r8] = (ve, hev)
                            up_last_ve = hev
                    ffree[b] = (pe, last_mm)

            B.wait("tensor", ve, up_last_ve)
            for dh in range(2):
                for r8_ in range(8):
                    B.waitp("tensor", pf8[r8_])

                def down_ldwfx(sl, dh=dh):
                    b = fslot[0] % 4; fslot[0] += 1
                    B.waitp("gpsimd", ffree[b])
                    src = d_wdT[dh, sl, :, :]
                    dst = wfsl[:, b * 4096:(b + 1) * 4096]
                    wld = GP.issue("gpsimd", lambda e, s=src, d=dst:
                                   e.dma_start(out=d, in_=s))
                    B.waitp("vector", wld)
                    B.waitp("vector", ffree[b])
                    B.wait("vector", ve, ve.n)
                    wfx = B.do("vector", lambda e, b=b: e.tensor_scalar_add(
                        wfsl[:, b * 4096:(b + 1) * 4096],
                        wfsl[:, b * 4096:(b + 1) * 4096], -128.0), ve, 1)
                    return b, wfx

                stop_pe = None
                curd = down_ldwfx(0)
                for sl in range(16):
                    b, wfx = curd
                    curd = down_ldwfx(sl + 1) if sl + 1 < 16 else None
                    B.wait("tensor", ve, wfx)
                    if sl == 2:
                        flush_tails()
                    for c4 in range(4):
                        for tile in range(4):
                            lhs = hbuf[:, (sl * 4 + c4) * TB + tile * 128:
                                       (sl * 4 + c4) * TB + tile * 128 + 128]
                            for h in range(2):
                                rhs = wfsl[:, b * 4096 + c4 * 1024 + h * 512:
                                           b * 4096 + c4 * 1024 + h * 512 + 512]
                                st = (sl == 0 and c4 == 0)
                                sp = (sl == 15 and c4 == 3)
                                B.do("tensor", lambda e,
                                     o=P[tile][:, h * 512:(h + 1) * 512],
                                     l=lhs, r=rhs, st=st, sp=sp:
                                     e.matmul(o, l, r, start=st, stop=sp))
                    slot_pe = B.inc_last("tensor", pe, 1)
                    ffree[b] = (pe, slot_pe)
                    if sl == 15:
                        stop_pe = slot_pe

                for tile in range(4):
                    n = chain_idx[0]; chain_idx[0] += 1
                    gt = tb * 4 + tile
                    B.wait("vector", pe, stop_pe)
                    B.waitp("vector", sd_ld)
                    B.wait("vector", ve, ve.n)
                    ev1 = B.do("vector", lambda e, tile=tile, dh=dh: e.tensor_mul(
                        P[tile][:, 0:1024], P[tile][:, 0:1024],
                        sdbc[:, dh * 1024:(dh + 1) * 1024]), ve, 1)
                    B.wait("scalar", ve, ev1)
                    B.wait("scalar", sc, sc.n)
                    if n >= 4:
                        B.wait("scalar", ve, ch_ev3[n - 4])
                    ch_ev2[n] = B.do("scalar", lambda e, tile=tile, n=n: e.activation(
                        gfin[:, (n % 4) * 1024:(n % 4) * 1024 + 1024],
                        P[tile][:, 0:1024], AF.Silu), sc, 1)
                    pf8[2 * tile] = (sc, ch_ev2[n])
                    pf8[2 * tile + 1] = (sc, ch_ev2[n])
                    tails.append(lambda n=n, gt=gt, dh=dh: emit_tail(n, gt, dh))

        flush_tails()

        for s in SP.sems + GP.sems:
            B.wait("sync", s, s.n)

        with nc.Block() as block:
            B.install(block)

    return nc


def _host_prep(L0_attn_x, L0_attn_Wq, L0_attn_Wk, L0_attn_Wv, L0_attn_Wo,
               L0_ffn_gate_w, L0_ffn_gate_scale, L0_ffn_up_w, L0_ffn_up_scale,
               L0_ffn_down_w, L0_ffn_down_scale):
    x = np.asarray(L0_attn_x, dtype=np.float32)
    def tile_attn128(w):
        wt = np.asarray(w, dtype=np.float32).T.reshape(16, 128, 16, 128)
        return np.ascontiguousarray(wt.transpose(2, 1, 0, 3).reshape(16, 128, 2048))

    def tile_attn256(w):
        wt = np.asarray(w, dtype=np.float32).T.reshape(16, 128, 8, 256)
        return np.ascontiguousarray(wt.transpose(2, 1, 0, 3).reshape(8, 128, 4096))

    wqT = tile_attn128(L0_attn_Wq)
    wkT = tile_attn128(L0_attn_Wk)
    wvT = tile_attn128(L0_attn_Wv)
    woT = tile_attn256(L0_attn_Wo)

    def tile_gate(w):
        wt = np.asarray(w).astype(np.uint8).T.reshape(16, 128, 32, 256)
        return np.ascontiguousarray(wt.transpose(2, 1, 0, 3).reshape(32, 128, 4096))

    wgT = tile_gate(L0_ffn_gate_w)
    wuT = tile_gate(L0_ffn_up_w)
    wdt = np.asarray(L0_ffn_down_w).astype(np.uint8).T.reshape(16, 4, 128, 2, 1024)
    wdT = np.ascontiguousarray(wdt.transpose(3, 0, 2, 1, 4).reshape(2, 16, 128, 4096))
    sg = np.ascontiguousarray(np.asarray(L0_ffn_gate_scale, dtype=np.float32)
                              .reshape(NIC, 128).T)
    su = np.ascontiguousarray(np.asarray(L0_ffn_up_scale, dtype=np.float32)
                              .reshape(NIC, 128).T)
    sd = np.ascontiguousarray(np.broadcast_to(
        np.asarray(L0_ffn_down_scale, dtype=np.float32), (128, D)))

    in_maps = []
    for c in range(8):
        b, j = c // 4, c % 4
        xs = np.ascontiguousarray(x[b, j * T:(j + 1) * T, :])
        mvec = np.zeros(8, dtype=np.float32)
        for r in range(8):
            if r // 4 == b and r % 4 < j:
                mvec[r] = 1.0
        mask = np.ascontiguousarray(np.broadcast_to(mvec, (128, 8)))
        in_maps.append(dict(x=xs, wqT=wqT, wkT=wkT, wvT=wvT, woT=woT,
                            wgT=wgT, wuT=wuT, wdT=wdT, sg=sg, su=su, sd=sd,
                            mask=mask))
    return in_maps


def kernel(**inputs):
    in_maps = _host_prep(**inputs)
    nc = _build_nc()
    res = run_bass_kernel_spmd(nc, in_maps, core_ids=list(range(8)))
    if DEBUG:
        kernel.last_results = res
    full = np.empty((2, 8192, D), dtype=np.float32)
    for c in range(8):
        b, j = c // 4, c % 4
        full[b, j * T:(j + 1) * T, :] = np.asarray(res.results[c]["out"],
                                                   dtype=np.float32)
    return full
